# version 05 (final)
# speedup vs baseline: 1.1282x; 1.0561x over previous
import numpy as np
import ml_dtypes
from contextlib import ExitStack

import concourse.bass as bass
import concourse.mybir as mybir
from concourse.bass_utils import run_bass_kernel_spmd

F32 = mybir.dt.float32
BF16 = mybir.dt.bfloat16
AF = mybir.ActivationFunctionType
ALU = mybir.AluOpType
AX = mybir.AxisListType
BFnp = ml_dtypes.bfloat16

MAGIC = float(np.float32(3 * 2 ** 22))
INV2PI = float(1.0 / (2.0 * np.pi))
TWO_PI = float(2.0 * np.pi)
D = 1024
SEQ = 4096
T = 2048
NH = 16
HARM = 32
DH = 4096
EPS_LN = 1e-5
NCH = D // 128
HCH = DH // 128
EXT = 256
LEN_L = 200
EXTW = 768
NE = 8
LEN_L = 200
TQ = T + EXT
XQW = T + 2 + EXTW

ENGINES = ("sync", "gpsimd", "tensor", "vector", "scalar")


class Sem:
    def __init__(self, nc, name):
        self.h = nc.semaphore(name).__enter__()
        self.n = 0

    def inc(self, k=1):
        self.n += k
        return self.n


class DStream:

    def __init__(self, P, name, n=2):
        self.sems = [P.sem(f"{name}{i}") for i in range(n)]
        self.k = 0

    def nxt(self):
        s = self.sems[self.k % len(self.sems)]
        self.k += 1
        return s, s.inc(16)

    def pend(self, P, eng):
        for s in self.sems:
            P.pend(eng, s)


class SelfEdge:

    def __init__(self, nc):
        self.sems = {en: Sem(nc, f"se_{en}") for en in
                     ("vector", "scalar", "gpsimd", "tensor")}

    def step(self, en, e, ins):
        s = self.sems[en]
        v = s.inc(1)
        ins.then_inc(s.h, 1)
        e.wait_ge(s.h, v)
        return ins


class Prog:

    def __init__(self, nc):
        self.nc = nc
        self.q = {e: [] for e in ENGINES}
        self.nbar = 0
        self.bar = Sem(nc, "bar")
        self.pending = {e: [] for e in ENGINES}

    def sem(self, name):
        return Sem(self.nc, name)

    def emit(self, eng, fn):
        self.q[eng].append(fn)

    def pend(self, eng, sem, val=None):
        self.pending[eng].append((sem, sem.n if val is None else val))

    def barrier(self):
        self.nbar += 1
        target = 5 * self.nbar
        bar = self.bar
        for e in ENGINES:
            waits = self.pending[e]
            self.pending[e] = []

            def f(eng, waits=waits, target=target, e_name=e):
                for (s, v) in waits:
                    if v > 0:
                        eng.wait_ge(s.h, v)
                if e_name != "sync":
                    eng.drain()
                eng.nop().then_inc(bar.h, 1)
                eng.wait_ge(bar.h, target)
            self.emit(e, f)
        bar.inc(5)

    def finalize(self):
        block = self.nc.Block().__enter__()
        for e in ENGINES:
            fns = self.q[e]

            def run_all(eng, fns=fns):
                for f in fns:
                    f(eng)
            getattr(block, e)(run_all)
        block.__exit__(None, None, None)


class Arena:

    def __init__(self, nc, base, size):
        self.nc = nc
        self.base = base
        self.size = size
        self.off = 0
        self.hi = 0
        self.n = 0

    def reset(self, off=0):
        self.off = off

    def carve(self, name, shape, dt):
        nbytes = int(np.prod(shape[1:])) * mybir.dt.size(dt)
        off = (self.off + 31) & ~31
        self.off = off + nbytes
        self.hi = max(self.hi, self.off)
        assert self.off <= self.size, (name, self.off, self.size)
        self.n += 1
        h = self.nc.alloc_sbuf_tensor_at(f"ar_{name}_{self.n}", shape, dt,
                                         offset=self.base + off)
        return h


def _mk_consts():
    return {
        "idb": np.eye(128, dtype=BFnp),
        "idf": np.eye(128, dtype=np.float32),
        "ones_f_row": np.ones((1, 128), dtype=np.float32),
    }


def build_fused(phases=("p0", "p1", "p2", "p3", "p4", "l1"), dbg=(), p2_heads=NH, p2_stage="full", p2_kc=HCH):
    half = 0
    nc = bass.Bass("TRN2", target_bir_lowering=False, debug=False)
    P = Prog(nc)

    x_seq = nc.dram_tensor("x_seq", [SEQ, D], F32, kind="ExternalInput").ap()
    idb_d = nc.dram_tensor("idb", [128, 128], BF16, kind="ExternalInput").ap()
    idf_d = nc.dram_tensor("idf", [128, 128], F32, kind="ExternalInput").ap()
    ones_d = nc.dram_tensor("ones_f_row", [1, 128], F32, kind="ExternalInput").ap()
    basT_d = {p: nc.dram_tensor(f"basT_{p}", [D, HARM], F32, kind="ExternalInput").ap()
              for p in "qkvo"}
    phT_d = {p: nc.dram_tensor(f"phT_{p}", [HARM, D], F32, kind="ExternalInput").ap()
             for p in "qkvo"}
    ampT_d = {p: nc.dram_tensor(f"ampT_{p}", [HARM, D], F32, kind="ExternalInput").ap()
              for p in "qkvo"}
    w1T_d = nc.dram_tensor("w1T", [D, DH], F32, kind="ExternalInput").ap()
    w2T_d = nc.dram_tensor("w2T", [DH, D], F32, kind="ExternalInput").ap()
    convw_d = nc.dram_tensor("convw", [128, 96], F32, kind="ExternalInput").ap()
    convb_d = nc.dram_tensor("convb", [128, 32], F32, kind="ExternalInput").ap()

    x_extra = nc.dram_tensor("x_extra", [EXTW, D], F32, kind="ExternalInput").ap()
    x_extra_fm = nc.dram_tensor("x_extra_fm", [128, NCH, EXT], F32,
                                kind="ExternalInput").ap()

    xq_hbm = nc.dram_tensor("xq_hbm", [128, NCH, XQW], BF16, kind="Internal").ap()
    w1q_hbm = nc.dram_tensor("w1q_hbm", [HCH, NCH, 128, 128], BF16, kind="Internal").ap()
    w2q_hbm = nc.dram_tensor("w2q_hbm", [HCH, 128, D], BF16, kind="Internal").ap()
    brow_m = nc.dram_tensor("brow_m", [18, 128], BF16, kind="Internal").ap()
    brow_f = nc.dram_tensor("brow_f", [24, 128], F32, kind="Internal").ap()

    dbg = set(dbg)
    dbg_out = {}

    s0 = T * half
    own_tiles = list(range(4 * half, 4 * half + 4))
    if half == 0:
        bq_tile, bq_quad, bq_part, bq_col = 4, 0, 0, T + 1
    else:
        bq_tile, bq_quad, bq_part, bq_col = 3, 3, 127, 0

    selq_d = nc.dram_tensor("selq", [128, NCH, 16], BF16, kind="ExternalInput").ap()
    selb_d = nc.dram_tensor("selb", [16, NCH, 128], F32, kind="ExternalInput").ap()
    wq_d = nc.dram_tensor("Wq", [D, D], F32, kind="ExternalInput").ap()
    wk_d = nc.dram_tensor("Wk", [D, D], F32, kind="ExternalInput").ap()
    wo_d = nc.dram_tensor("Wo", [D, D], F32, kind="ExternalInput").ap()
    bq_d = nc.dram_tensor("bq", [128, NCH], F32, kind="ExternalInput").ap()
    bk_d = nc.dram_tensor("bk", [128, NCH], F32, kind="ExternalInput").ap()
    bo_d = nc.dram_tensor("bo", [128, NCH], F32, kind="ExternalInput").ap()
    bc2_d = nc.dram_tensor("bc2", [128, 1], F32, kind="ExternalInput").ap()
    WeT_d = nc.dram_tensor("WeT", [LEN_L, 16, 20], F32, kind="ExternalInput").ap()
    WrT_d = nc.dram_tensor("WrT", [LEN_L, 16, 20], F32, kind="ExternalInput").ap()
    wct_d = nc.dram_tensor("Wc_top", [20, 64], F32, kind="ExternalInput").ap()
    wcb_d = nc.dram_tensor("Wc_bot", [20, 64], F32, kind="ExternalInput").ap()
    wrt_d = nc.dram_tensor("w_router", [D, 8], F32, kind="ExternalInput").ap()
    wgt_d = nc.dram_tensor("w_gating", [D, 8], F32, kind="ExternalInput").ap()
    wgate_d = nc.dram_tensor("w_gate", [8, D, DH], F32, kind="ExternalInput").ap()
    wup_d = nc.dram_tensor("w_up", [8, D, DH], F32, kind="ExternalInput").ap()
    wdown_d = nc.dram_tensor("w_down", [8, DH, D], F32, kind="ExternalInput").ap()
    x2_out = nc.dram_tensor("x2_fm", [128, NCH, T], F32, kind="ExternalOutput").ap()
    gate_bf = nc.dram_tensor("gate_bf", [8, D, DH], BF16, kind="Internal").ap()
    up_bf = nc.dram_tensor("up_bf", [8, D, DH], BF16, kind="Internal").ap()
    down_bf = nc.dram_tensor("down_bf", [8, DH, D], BF16, kind="Internal").ap()
    wqkv_bf = nc.dram_tensor("wqkv_bf", [3, D, D], BF16, kind="Internal").ap()

    def sb(name, shape, dt):
        return nc.alloc_sbuf_tensor(name, shape, dt)

    x_res = sb("x_res", [128, NCH, T], F32)
    x_land = sb("x_land", [128, NCH, EXT], F32)
    f_row = sb("f_row", [1, XQW], F32)
    id_b = sb("id_b", [128, 128], BF16)
    id_f = sb("id_f", [128, 128], F32)
    ones_f = sb("ones_f", [1, 128], F32)
    ones_bfc = sb("ones_bfc", [128, 1], BF16)
    basT = {p: sb(f"basTs_{p}", [128, NCH, HARM], BF16) for p in "qkvo"}
    wT = {p: sb(f"wTs_{p}", [32, D], BF16) for p in "qkvo"}
    convw = sb("convw_s", [128, 32, 3], F32)
    convb = sb("convb_s", [128, 32, 1], F32)
    mc_col = sb("mc_col", [128, 1], F32)
    mc2 = sb("mc2", [1, 1], F32)
    swcols = sb("swcols", [128, 4], F32)
    fcol_b = sb("fcol_b", [128, 1], F32)
    frow16 = sb("frow16", [24, 128], F32)
    rows = sb("rows", [1, 256], F32)
    eps_col = sb("eps_col", [128, 1], F32)

    selq = sb("selq_s", [128, NCH, 16], BF16)
    selb = sb("selb_s", [16, NCH, 128], F32)
    bq_sb = sb("bq_sb", [128, NCH], F32)
    bk_sb = sb("bk_sb", [128, NCH], F32)
    bo_sb = sb("bo_sb", [128, NCH], F32)
    bc2 = sb("bc2_s", [128, 1], F32)
    WeT1 = sb("WeT1_s", [128, 16, 20], BF16)
    WeT2 = sb("WeT2_s", [72, 16, 20], BF16)
    WrT1 = sb("WrT1_s", [128, 16, 20], BF16)
    WrT2 = sb("WrT2_s", [72, 16, 20], BF16)
    wct = sb("wct_s", [20, 64], BF16)
    wcb = sb("wcb_s", [20, 64], BF16)
    wr8 = sb("wr8", [128, NCH, 8], BF16)
    wg8 = sb("wg8", [128, NCH, 8], BF16)
    weff = sb("weff", [128, NCH, 2, 64], BF16)
    cw_fm = sb("cw_fm", [8, T], F32)

    base = (nc.sbuf_base + 31) & ~31
    cap = nc.sbuf_top - base
    AR = Arena(nc, base, cap)

    ex = ExitStack()
    psAll_g = ex.enter_context(nc.psum_tensor("psAll", [128, 4096], F32))
    psAll = psAll_g[:]
    psA = psAll[:, 0:512]
    psB = psAll[:, 512:1024]
    psC = psAll[:, 1024:2048]
    psD = psAll[:, 2048:3072]
    psE = psAll_g[0:65, 3072:4096]
    psAb = psA.bitcast(BF16)
    psBb = psB.bitcast(BF16)
    psDb = psD.bitcast(BF16)
    psb16 = psAll.bitcast(BF16)
    psD8 = psAll.rearrange("p (a q) -> p a q", q=512)

    ld_c = DStream(P, "ld_c", 2)
    ld_g = P.sem("ld_g")
    ld_x = DStream(P, "ld_x", 2)
    ld_w = DStream(P, "ld_w", 2)
    st_w = DStream(P, "st_w", 2)
    st_z = DStream(P, "st_z", 2)
    ld_s = DStream(P, "ld_s", 3)
    ld_s2 = DStream(P, "ld_s2", 3)
    ld_q = DStream(P, "ld_q", 2)
    dma_m = DStream(P, "dma_m", 2)
    dma_rs = DStream(P, "dma_rs", 2)
    dma_od = DStream(P, "dma_od", 2)
    ld_c2 = DStream(P, "ld_c2", 14)
    ld_g2 = P.sem("ld_g2")
    ld_w2 = DStream(P, "ld_w2", 3)
    ld_m = DStream(P, "ld_m", 3)
    ld_d = DStream(P, "ld_d", 3)
    ld_dn = DStream(P, "ld_dn", 3)
    dma_sh = DStream(P, "dma_sh", 2)
    cast_s = [P.sem(f"cast{e}") for e in range(8)]
    cast_w = P.sem("cast_w")
    sv = P.sem("sv")
    sa = P.sem("sa")
    sm = P.sem("sm")
    sg = P.sem("sg")

    state = {}
    SE = SelfEdge(nc)

    def vstep(e, ins):
        return SE.step("vector", e, ins)

    def p0():
        AR.reset()
        ph_st = AR.carve("ph_st", [32, D], F32)
        am_st = AR.carve("am_st", [32, D], F32)
        u_st = AR.carve("u_st", [32, D], F32)
        k_st = AR.carve("k_st", [32, D], F32)
        wst = AR.carve("wst", [128, 2, DH], F32)
        sums = AR.carve("sums", [128, 40], F32)
        wqb = AR.carve("wqb", [128, DH], BF16)
        sca = AR.carve("sca", [1, 8], F32)
        ones_fc = AR.carve("ones_fc", [128, 1], F32)

        def f_v_eps(e):
            e.memset(eps_col[:], EPS_LN).then_inc(sv.h, 1)
        sv.inc(1)
        P.emit("vector", f_v_eps)

        cw = [ld_c.nxt() for _ in range(5)]

        def f_sync(e, cw=cw):
            e.dma_start(out=id_b[:], in_=idb_d).then_inc(cw[0][0].h, 16)
            e.dma_start(out=id_f[:], in_=idf_d).then_inc(cw[1][0].h, 16)
            e.dma_start(out=ones_f[:], in_=ones_d).then_inc(cw[2][0].h, 16)
            e.dma_start(out=convw[:],
                        in_=convw_d.rearrange("p (c t) -> p c t", t=3)).then_inc(cw[3][0].h, 16)
            e.dma_start(out=convb[:], in_=convb_d.unsqueeze(2)).then_inc(cw[4][0].h, 16)
        P.emit("sync", f_sync)

        def f_g(e):
            for p in "qkvo":
                e.dma_start(out=basT[p][:],
                            in_=basT_d[p].rearrange("(c p) h -> p c h", p=128)
                            ).then_inc(ld_g.h, 16)
        P.emit("gpsimd", f_g)
        ld_g.inc(16 * 4)

        for i, p in enumerate("qkvo"):
            svw = sv.n

            w1_ = ld_c.nxt()
            w2_ = ld_c.nxt()

            def f_sy(e, p=p, svw=svw, i=i, w1_=w1_, w2_=w2_):
                if i > 0:
                    e.wait_ge(sv.h, svw)
                e.dma_start(out=ph_st[:], in_=phT_d[p]).then_inc(w1_[0].h, 16)
                e.dma_start(out=am_st[:], in_=ampT_d[p]).then_inc(w2_[0].h, 16)
            P.emit("sync", f_sy)

            def f_v(e, w1_=w1_, w2_=w2_):
                e.wait_ge(w1_[0].h, w1_[1])
                e.wait_ge(w2_[0].h, w2_[1])
                vstep(e, e.tensor_scalar(u_st[:], ph_st[:], INV2PI, 0.25,
                                         ALU.mult, ALU.add))
                vstep(e, e.tensor_scalar(k_st[:], u_st[:], MAGIC, MAGIC,
                                         ALU.add, ALU.subtract))
                e.tensor_tensor(u_st[:], u_st[:], k_st[:], ALU.subtract).then_inc(sv.h, 1)
            svs = sv.inc(1)
            P.emit("vector", f_v)

            def f_s(e, svs=svs):
                e.wait_ge(sv.h, svs)
                e.activation(k_st[:], u_st[:], AF.Sin, bias=0.0,
                             scale=TWO_PI).then_inc(sa.h, 1)
            sas = sa.inc(1)
            P.emit("scalar", f_s)

            def f_v2(e, p=p, sas=sas):
                e.wait_ge(sa.h, sas)
                if p == "q":
                    e.scalar_tensor_tensor(wT[p][:], k_st[:], 0.125, am_st[:],
                                           ALU.mult, ALU.mult).then_inc(sv.h, 1)
                else:
                    e.tensor_tensor(wT[p][:], k_st[:], am_st[:],
                                    ALU.mult).then_inc(sv.h, 1)
            sv.inc(1)
            P.emit("vector", f_v2)

        wspec = [(w1T_d, NCH, DH), (w2T_d, HCH, D)]
        red_ms = []
        k = 0
        for wi, (wt, nch, wid) in enumerate(wspec):
            for c in range(nch):
                buf = k % 2
                guard = red_ms[k - 2] if k >= 2 else 0

                lw = ld_w.nxt()

                def f_sy(e, wt=wt, c=c, buf=buf, wid=wid, guard=guard, lw=lw):
                    if guard:
                        e.wait_ge(sv.h, guard)
                    e.dma_start(out=wst[:, buf, 0:wid],
                                in_=wt[c * 128:(c + 1) * 128, :]).then_inc(lw[0].h, 16)
                P.emit("sync", f_sy)

                col = wi * NCH + c

                def f_v3(e, col=col, buf=buf, wid=wid, lw=lw):
                    e.wait_ge(lw[0].h, lw[1])
                    e.reduce_sum(sums[:, col:col + 1], wst[:, buf, 0:wid],
                                 axis=AX.X,
                                 apply_absolute_value=True).then_inc(sv.h, 1)
                red_ms.append(sv.inc(1))
                P.emit("vector", f_v3)
                k += 1

        def f_v4(e):
            e.memset(ones_bfc[:], 1.0)
            e.memset(ones_fc[:], 1.0).then_inc(sv.h, 1)
        svo = sv.inc(1)
        P.emit("vector", f_v4)

        def f_t(e, svo=svo):
            e.wait_ge(sv.h, svo)
            e.matmul(psA[0:1, 0:40], ones_fc[:], sums[:, 0:40],
                     start=True, stop=True).then_inc(sm.h, 1)
        smv = sm.inc(1)
        P.emit("tensor", f_t)

        def f_v5(e, smv=smv):
            e.wait_ge(sm.h, smv)
            e.reduce_sum(sca[0:1, 0:1], psA[0:1, 0:NCH], axis=AX.X)
            vstep(e, e.reduce_sum(sca[0:1, 1:2], psA[0:1, NCH:NCH + HCH], axis=AX.X))
            vstep(e, e.tensor_scalar(sca[0:1, 0:2], sca[0:1, 0:2], 1.0 / (D * DH),
                                     1e-5, ALU.mult, ALU.max))
            vstep(e, e.tensor_scalar(sca[0:1, 2:4], sca[0:1, 0:2], 1.0 / 127.0,
                                     None, ALU.mult))
            vstep(e, e.tensor_copy(mc2[:], sca[0:1, 3:4]))
            e.reciprocal(sca[0:1, 0:2], sca[0:1, 0:2]).then_inc(sv.h, 1)
        svr = sv.inc(1)
        P.emit("vector", f_v5)

        def f_t2(e, svr=svr):
            e.wait_ge(sv.h, svr)
            e.matmul(psB[:, 0:4], ones_f[:], sca[0:1, 0:4],
                     start=True, stop=True).then_inc(sm.h, 1)
        smv2 = sm.inc(1)
        P.emit("tensor", f_t2)

        def f_v6(e, smv2=smv2):
            e.wait_ge(sm.h, smv2)
            vstep(e, e.tensor_copy(swcols[:], psB[:, 0:4]))
            e.tensor_copy(mc_col[:], psB[:, 2:3]).then_inc(sv.h, 1)
        svz = sv.inc(1)
        P.emit("vector", f_v6)

        k = 0
        q_ms = []
        for wi, (wt, nch, wid) in enumerate(wspec):
            for c in range(nch):
                buf = k % 2
                guard = q_ms[k - 2] if k >= 2 else svz

                lw = ld_w.nxt()

                def f_sy2(e, wt=wt, c=c, buf=buf, wid=wid, guard=guard, lw=lw):
                    e.wait_ge(sv.h, guard)
                    e.dma_start(out=wst[:, buf, 0:wid],
                                in_=wt[c * 128:(c + 1) * 128, :]).then_inc(lw[0].h, 16)
                P.emit("sync", f_sy2)

                stw_guard = getattr(p0, "_last_store", None)

                def f_v7(e, buf=buf, wid=wid, lw=lw, wi=wi, stw_guard=stw_guard):
                    e.wait_ge(lw[0].h, lw[1])
                    if stw_guard:
                        e.wait_ge(stw_guard[0].h, stw_guard[1])
                    sl = swcols[:, wi:wi + 1]
                    vstep(e, e.tensor_scalar(wst[:, buf, 0:wid], wst[:, buf, 0:wid],
                                             sl, None, ALU.mult))
                    vstep(e, e.tensor_scalar(wst[:, buf, 0:wid], wst[:, buf, 0:wid],
                                             MAGIC, MAGIC, ALU.add, ALU.subtract))
                    e.tensor_scalar(wqb[:, 0:wid], wst[:, buf, 0:wid],
                                    1.0, -1.0, ALU.min, ALU.max).then_inc(sv.h, 1)
                svv = sv.inc(1)
                q_ms.append(svv)
                P.emit("vector", f_v7)

                sw = st_w.nxt()
                p0._last_store = sw

                def f_g2(e, c=c, wid=wid, wi=wi, svv=svv, sw=sw):
                    e.wait_ge(sv.h, svv)
                    if wi == 0:
                        e.dma_start(
                            out=w1q_hbm[:, c].rearrange("hc p q -> p hc q"),
                            in_=wqb[:].rearrange("p (hc q) -> p hc q", q=128),
                        ).then_inc(sw[0].h, 16)
                    else:
                        e.dma_start(out=w2q_hbm[c], in_=wqb[:, 0:wid]).then_inc(sw[0].h, 16)
                P.emit("gpsimd", f_g2)
                k += 1

        state["sums"] = sums
        state["sca"] = sca
        state["ld_c_fin"] = [(s, s.n) for s in ld_c.sems]
        state["ld_g_total"] = ld_g.n
        ld_w.pend(P, "sync")
        st_w.pend(P, "gpsimd")
        P.pend("vector", sv)
        P.pend("scalar", sa)
        P.pend("tensor", sm)

    def p1():
        AR.reset()
        res = {p: AR.carve(f"res_{p}", [32, SEQ], BF16) for p in "qkv"}
        res_qx = AR.carve("res_qx", [32, EXT], BF16)
        state["res"] = res
        state["res_qx"] = res_qx
        state["res_end"] = AR.off
        xt = AR.carve("xt", [128, 1, 4, D], F32)
        sq = AR.carve("sq", [128, D], F32)
        xnq = AR.carve("xnq", [128, 2, D], BF16)
        xqq = AR.carve("xqq", [128, D], F32)
        xqb = AR.carve("xqb", [128, D], BF16)
        xn_fm = AR.carve("xn_fm", [128, NCH, 512], BF16)
        xqb_t = AR.carve("xqb_t", [128, NCH, 128], BF16)
        xq_stg = AR.carve("xq_stg", [128, 2, NCH, 128], BF16)
        state["xqb_t"] = xqb_t
        stt = AR.carve("stt", [128, 12], F32)
        fcols = AR.carve("fcols", [128, 24], F32)
        state["p1_end"] = AR.off

        def f_v0(e):
            e.memset(f_row[:], 1.0)
            e.memset(fcols[:], 0.0)
            e.memset(xq_stg[:, 0, :, 0:1], 0.0).then_inc(sv.h, 1)
        svz0 = sv.inc(1)
        P.emit("vector", f_v0)
        z_ = st_z.nxt()

        def f_gz(e, svz0=svz0, z_=z_):
            e.wait_ge(sv.h, svz0)
            with nc.allow_non_contiguous_dma(reason="one-time 1-col pad"):
                e.dma_start(out=xq_hbm[:, :, 0:1],
                            in_=xq_stg[:, 0, :, 0:1]).then_inc(z_[0].h, 16)
        P.emit("gpsimd", f_gz)
        state["xq_z"] = z_

        xt_done = {}
        xq_guard = {0: z_, 1: None}
        sm_xn_hist = {}

        for j in range(8):
            buf = 0
            guard = xt_done.get(j - 1, 0)

            lx = ld_x.nxt()

            def f_sy(e, j=j, buf=buf, guard=guard, lx=lx):
                if guard:
                    e.wait_ge(sv.h, guard)
                e.dma_start(out=xt[:, buf],
                            in_=x_seq[512 * j:512 * (j + 1), :]
                            .rearrange("(a p) f -> p a f", p=128)).then_inc(lx[0].h, 16)
            P.emit("sync", f_sy)

            for a in range(4):
                qi = 4 * j + a
                is_own = j in own_tiles
                m = (j - 4 * half) * 4 + a if is_own else -1
                is_bq = (j == bq_tile and a == bq_quad)
                xin = xt[:, buf, a, :]

                sq_guard = sv.n

                def f_sq(e, xin=xin, lx=lx, sq_guard=sq_guard):
                    e.wait_ge(lx[0].h, lx[1])
                    if sq_guard:
                        e.wait_ge(sv.h, sq_guard)
                    e.activation(sq[:], xin, AF.Square).then_inc(sa.h, 1)
                sav = sa.inc(1)
                P.emit("scalar", f_sq)

                def f_st(e, xin=xin, lx=lx, sav=sav):
                    e.wait_ge(lx[0].h, lx[1])
                    e.reduce_sum(stt[:, 0:1], xin, axis=AX.X)
                    e.wait_ge(sa.h, sav)
                    vstep(e, e.reduce_sum(stt[:, 1:2], sq[:], axis=AX.X))
                    vstep(e, e.tensor_scalar(stt[:, 0:2], stt[:, 0:2], 1.0 / D,
                                             None, ALU.mult))
                    vstep(e, e.tensor_tensor(stt[:, 2:3], stt[:, 0:1], stt[:, 0:1],
                                             ALU.mult))
                    e.tensor_tensor(stt[:, 2:3], stt[:, 1:2], stt[:, 2:3],
                                    ALU.subtract).then_inc(sv.h, 1)
                svv = sv.inc(1)
                P.emit("vector", f_st)

                def f_sd(e, svv=svv):
                    e.wait_ge(sv.h, svv)
                    e.activation(stt[:, 3:4], stt[:, 2:3], AF.Sqrt,
                                 bias=eps_col[:]).then_inc(sa.h, 1)
                sav2 = sa.inc(1)
                P.emit("scalar", f_sd)

                smg = sm.n

                def f_ap(e, xin=xin, sav2=sav2, a=a, m=m, smg=smg,
                         is_own=is_own, is_bq=is_bq):
                    e.wait_ge(sa.h, sav2)
                    if smg:
                        e.wait_ge(sm.h, smg)
                    vstep(e, e.reciprocal(stt[:, 4:5], stt[:, 3:4]))
                    vstep(e, e.tensor_tensor(stt[:, 5:6], stt[:, 0:1], stt[:, 4:5],
                                             ALU.mult))
                    e.tensor_scalar(xnq[:, a % 2, :], xin, stt[:, 4:5], stt[:, 5:6],
                                    ALU.mult, ALU.subtract)
                    if is_own or is_bq:
                        vstep(e, e.reduce_max(stt[:, 6:7], xin, axis=AX.X))
                        vstep(e, e.tensor_reduce(stt[:, 7:8], xin, axis=AX.X,
                                                 op=ALU.min))
                        vstep(e, e.tensor_tensor(stt[:, 6:7], stt[:, 6:7], stt[:, 0:1],
                                                 ALU.subtract))
                        vstep(e, e.tensor_tensor(stt[:, 7:8], stt[:, 0:1], stt[:, 7:8],
                                                 ALU.subtract))
                        vstep(e, e.tensor_tensor(stt[:, 6:7], stt[:, 6:7], stt[:, 7:8],
                                                 ALU.max))
                        vstep(e, e.tensor_tensor(stt[:, 6:7], stt[:, 6:7], stt[:, 4:5],
                                                 ALU.mult))
                        vstep(e, e.tensor_scalar(stt[:, 6:7], stt[:, 6:7], 1e-5,
                                                 None, ALU.max))
                        vstep(e, e.reciprocal(stt[:, 8:9], stt[:, 6:7]))
                        vstep(e, e.tensor_scalar(stt[:, 8:9], stt[:, 8:9], 127.0,
                                                 None, ALU.mult))
                        vstep(e, e.tensor_tensor(stt[:, 9:10], stt[:, 4:5], stt[:, 8:9],
                                                 ALU.mult))
                        vstep(e, e.tensor_tensor(stt[:, 10:11], stt[:, 5:6], stt[:, 8:9],
                                                 ALU.mult))
                        vstep(e, e.tensor_scalar(xqq[:], xin, stt[:, 9:10],
                                                 stt[:, 10:11], ALU.mult, ALU.subtract))
                        vstep(e, e.tensor_scalar(xqq[:], xqq[:], MAGIC, MAGIC,
                                                 ALU.add, ALU.subtract))
                        e.tensor_scalar(xqb[:], xqq[:], 127.0, -128.0, ALU.min, ALU.max)
                        vstep(e, e.tensor_tensor(stt[:, 11:12], stt[:, 6:7], mc_col[:],
                                                 ALU.mult))
                        if is_own:
                            e.tensor_copy(fcols[:, m:m + 1], stt[:, 11:12])
                        if is_bq:
                            e.tensor_copy(fcol_b[:], stt[:, 11:12])
                    e.drain()
                    e.nop().then_inc(sv.h, 1)
                svap = sv.inc(1)
                P.emit("vector", f_ap)

                def f_tp(e, a=a, xin=xin, svap=svap, is_own=is_own, is_bq=is_bq):
                    e.wait_ge(sv.h, svap)
                    pb = psAb if a % 2 == 0 else psBb
                    for c in range(NCH):
                        ins = e.transpose(pb[:, c * 128:(c + 1) * 128],
                                          xnq[:, a % 2, c * 128:(c + 1) * 128], id_b[:])
                    ins.then_inc(sm.h, 1)
                    if is_own:
                        for c in range(NCH):
                            ins = e.transpose(psC[:, c * 128:(c + 1) * 128],
                                              xin[:, c * 128:(c + 1) * 128], id_f[:])
                        ins.then_inc(sm.h, 1)
                    if is_own or is_bq:
                        for c in range(NCH):
                            ins = e.transpose(psDb[:, c * 128:(c + 1) * 128],
                                              xqb[:, c * 128:(c + 1) * 128], id_b[:])
                        ins.then_inc(sm.h, 1)
                P.emit("tensor", f_tp)
                sm_xn = sm.inc(1)
                sm_x0 = sm.inc(1) if is_own else 0
                sm_xq = sm.inc(1) if (is_own or is_bq) else 0

                def f_ev(e, a=a, m=m, is_own=is_own, is_bq=is_bq,
                         sm_xn=sm_xn, sm_x0=sm_x0, sm_xq=sm_xq,
                         xg=xq_guard[m % 2] if m >= 0 else None):
                    pb = psAb if a % 2 == 0 else psBb
                    e.wait_ge(sm.h, sm_xn)
                    e.tensor_copy(xn_fm[:, :, 128 * a:128 * (a + 1)],
                                  pb.rearrange("p (c q) -> p c q", q=128))
                    if is_own:
                        e.wait_ge(sm.h, sm_x0)
                        e.tensor_copy(x_res[:, :, 128 * m:128 * (m + 1)],
                                      psC[:].rearrange("p (c q) -> p c q", q=128))
                        e.wait_ge(sm.h, sm_xq)
                        if xg:
                            e.wait_ge(xg[0].h, xg[1])
                        e.tensor_copy(xq_stg[:, m % 2],
                                      psDb[:, 0:1024].rearrange("p (c q) -> p c q", q=128))
                    elif is_bq:
                        e.wait_ge(sm.h, sm_xq)
                        vstep(e, e.tensor_copy(
                            xqb_t[:], psDb[:, 0:1024].rearrange("p (c q) -> p c q", q=128)))
                    e.drain()
                    e.nop().then_inc(sv.h, 1)
                sve2 = sv.inc(1)
                P.emit("vector", f_ev)

                if is_own:
                    sq_ = st_w.nxt()
                    xq_guard[m % 2] = sq_

                    def f_gx(e, m=m, sve2=sve2, sq_=sq_):
                        e.wait_ge(sv.h, sve2)
                        e.dma_start(out=xq_hbm[:, :, 1 + 128 * m:1 + 128 * (m + 1)],
                                    in_=xq_stg[:, m % 2]).then_inc(sq_[0].h, 16)
                    P.emit("gpsimd", f_gx)
                elif is_bq:
                    sq_ = st_z.nxt()

                    def f_gb2(e, sve2=sve2, sq_=sq_):
                        e.wait_ge(sv.h, sve2)
                        with nc.allow_non_contiguous_dma(reason="one-time 1-col"):
                            e.dma_start(out=xq_hbm[:, :, bq_col:bq_col + 1],
                                        in_=xqb_t[:, :, bq_part:bq_part + 1]
                                        ).then_inc(sq_[0].h, 16)
                    P.emit("gpsimd", f_gb2)
                    state["xq_bq"] = sq_

            xt_done[j] = sv.n

            svr2 = sv.n

            def f_res(e, j=j, svr2=svr2):
                e.wait_ge(sv.h, svr2)
                for (s_, v_) in state["ld_c_fin"]:
                    e.wait_ge(s_.h, v_)
                e.wait_ge(ld_g.h, state["ld_g_total"])
                for pi, p in enumerate("qkv"):
                    tgt = (psE[0:32, 0:512] if pi == 0 else
                           psE[0:32, 512:1024] if pi == 1 else psC[0:32, 0:512])
                    for c in range(NCH):
                        ins = e.matmul(tgt, basT[p][:, c, :], xn_fm[:, c, :],
                                       start=(c == 0), stop=(c == NCH - 1))
                ins.then_inc(sm.h, 1)
            smr = sm.inc(1)
            P.emit("tensor", f_res)

            def f_rese(e, j=j, smr=smr):
                e.wait_ge(sm.h, smr)
                e.tensor_copy(res["q"][:, 512 * j:512 * (j + 1)], psE[0:32, 0:512])
                e.tensor_copy(res["k"][:, 512 * j:512 * (j + 1)], psE[0:32, 512:1024])
                e.tensor_copy(res["v"][:, 512 * j:512 * (j + 1)],
                              psC[0:32, 0:512]).then_inc(sv.h, 1)
            sv.inc(1)
            P.emit("vector", f_rese)

        xle = [ld_x.nxt(), ld_x.nxt()]
        gv0 = sv.n

        def f_sye(e, xle=xle, gv0=gv0):
            e.wait_ge(sv.h, gv0)
            e.dma_start(out=xt[:, 0],
                        in_=x_extra[0:512].rearrange("(a p) f -> p a f", p=128)
                        ).then_inc(xle[0][0].h, 16)
        P.emit("sync", f_sye)

        for qx in range(6):
            if qx == 4:
                def f_sye2(e, xle=xle, gv1=sv.n):
                    e.wait_ge(sv.h, gv1)
                    e.dma_start(out=xt[:, 0, 0:2],
                                in_=x_extra[512:768]
                                .rearrange("(a p) f -> p a f", p=128)
                                ).then_inc(xle[1][0].h, 16)
                P.emit("sync", f_sye2)
            a = qx if qx < 4 else qx - 4
            xin = xt[:, 0, a, :]
            lxq = xle[0] if qx < 4 else xle[1]
            is_L = qx < 2

            sq_guard = sv.n

            def f_sqx(e, xin=xin, lxq=lxq, sq_guard=sq_guard):
                e.wait_ge(lxq[0].h, lxq[1])
                if sq_guard:
                    e.wait_ge(sv.h, sq_guard)
                e.activation(sq[:], xin, AF.Square).then_inc(sa.h, 1)
            sav = sa.inc(1)
            P.emit("scalar", f_sqx)

            def f_stx(e, xin=xin, lxq=lxq, sav=sav):
                e.wait_ge(lxq[0].h, lxq[1])
                e.reduce_sum(stt[:, 0:1], xin, axis=AX.X)
                e.wait_ge(sa.h, sav)
                vstep(e, e.reduce_sum(stt[:, 1:2], sq[:], axis=AX.X))
                vstep(e, e.tensor_scalar(stt[:, 0:2], stt[:, 0:2], 1.0 / D,
                                         None, ALU.mult))
                vstep(e, e.tensor_tensor(stt[:, 2:3], stt[:, 0:1], stt[:, 0:1],
                                         ALU.mult))
                e.tensor_tensor(stt[:, 2:3], stt[:, 1:2], stt[:, 2:3],
                                ALU.subtract).then_inc(sv.h, 1)
            svv = sv.inc(1)
            P.emit("vector", f_stx)

            def f_sdx(e, svv=svv):
                e.wait_ge(sv.h, svv)
                e.activation(stt[:, 3:4], stt[:, 2:3], AF.Sqrt,
                             bias=eps_col[:]).then_inc(sa.h, 1)
            sav2 = sa.inc(1)
            P.emit("scalar", f_sdx)

            smg = sm.n

            def f_apx(e, xin=xin, sav2=sav2, a=a, qx=qx, smg=smg, is_L=is_L):
                e.wait_ge(sa.h, sav2)
                if smg:
                    e.wait_ge(sm.h, smg)
                vstep(e, e.reciprocal(stt[:, 4:5], stt[:, 3:4]))
                vstep(e, e.tensor_tensor(stt[:, 5:6], stt[:, 0:1], stt[:, 4:5],
                                         ALU.mult))
                e.tensor_scalar(xnq[:, qx % 2, :], xin, stt[:, 4:5], stt[:, 5:6],
                                ALU.mult, ALU.subtract)
                vstep(e, e.reduce_max(stt[:, 6:7], xin, axis=AX.X))
                vstep(e, e.tensor_reduce(stt[:, 7:8], xin, axis=AX.X, op=ALU.min))
                vstep(e, e.tensor_tensor(stt[:, 6:7], stt[:, 6:7], stt[:, 0:1],
                                         ALU.subtract))
                vstep(e, e.tensor_tensor(stt[:, 7:8], stt[:, 0:1], stt[:, 7:8],
                                         ALU.subtract))
                vstep(e, e.tensor_tensor(stt[:, 6:7], stt[:, 6:7], stt[:, 7:8],
                                         ALU.max))
                vstep(e, e.tensor_tensor(stt[:, 6:7], stt[:, 6:7], stt[:, 4:5],
                                         ALU.mult))
                vstep(e, e.tensor_scalar(stt[:, 6:7], stt[:, 6:7], 1e-5, None,
                                         ALU.max))
                vstep(e, e.reciprocal(stt[:, 8:9], stt[:, 6:7]))
                vstep(e, e.tensor_scalar(stt[:, 8:9], stt[:, 8:9], 127.0, None,
                                         ALU.mult))
                vstep(e, e.tensor_tensor(stt[:, 9:10], stt[:, 4:5], stt[:, 8:9],
                                         ALU.mult))
                vstep(e, e.tensor_tensor(stt[:, 10:11], stt[:, 5:6], stt[:, 8:9],
                                         ALU.mult))
                vstep(e, e.tensor_scalar(xqq[:], xin, stt[:, 9:10], stt[:, 10:11],
                                         ALU.mult, ALU.subtract))
                vstep(e, e.tensor_scalar(xqq[:], xqq[:], MAGIC, MAGIC, ALU.add,
                                         ALU.subtract))
                e.tensor_scalar(xqb[:], xqq[:], 127.0, -128.0, ALU.min, ALU.max)
                vstep(e, e.tensor_tensor(stt[:, 11:12], stt[:, 6:7], mc_col[:],
                                         ALU.mult))
                e.tensor_copy(fcols[:, 16 + qx:17 + qx], stt[:, 11:12])
                e.drain()
                e.nop().then_inc(sv.h, 1)
            svap = sv.inc(1)
            P.emit("vector", f_apx)

            def f_tpx(e, qx=qx, svap=svap, is_L=is_L):
                e.wait_ge(sv.h, svap)
                if is_L:
                    pb = psAb if qx % 2 == 0 else psBb
                    for c in range(NCH):
                        ins = e.transpose(pb[:, c * 128:(c + 1) * 128],
                                          xnq[:, qx % 2, c * 128:(c + 1) * 128],
                                          id_b[:])
                    ins.then_inc(sm.h, 1)
                for c in range(NCH):
                    ins = e.transpose(psDb[:, c * 128:(c + 1) * 128],
                                      xqb[:, c * 128:(c + 1) * 128], id_b[:])
                ins.then_inc(sm.h, 1)
            P.emit("tensor", f_tpx)
            sm_xn = sm.inc(1) if is_L else 0
            sm_xq = sm.inc(1)

            def f_evx(e, qx=qx, is_L=is_L, sm_xn=sm_xn, sm_xq=sm_xq,
                      xg=xq_guard[qx % 2]):
                pb = psAb if qx % 2 == 0 else psBb
                if is_L:
                    e.wait_ge(sm.h, sm_xn)
                    e.tensor_copy(xn_fm[:, :, 128 * qx:128 * (qx + 1)],
                                  pb.rearrange("p (c q) -> p c q", q=128))
                e.wait_ge(sm.h, sm_xq)
                if xg:
                    e.wait_ge(xg[0].h, xg[1])
                e.tensor_copy(xq_stg[:, qx % 2],
                              psDb[:, 0:1024].rearrange("p (c q) -> p c q", q=128))
                e.drain()
                e.nop().then_inc(sv.h, 1)
            sve3 = sv.inc(1)
            P.emit("vector", f_evx)

            sq_ = st_w.nxt()
            xq_guard[qx % 2] = sq_

            def f_gxx(e, qx=qx, sve3=sve3, sq_=sq_):
                e.wait_ge(sv.h, sve3)
                e.dma_start(out=xq_hbm[:, :, T + 2 + 128 * qx:T + 2 + 128 * (qx + 1)],
                            in_=xq_stg[:, qx % 2]).then_inc(sq_[0].h, 16)
            P.emit("gpsimd", f_gxx)

        svrx = sv.n

        def f_resx(e, svrx=svrx):
            e.wait_ge(sv.h, svrx)
            for c in range(NCH):
                ins = e.matmul(psE[0:32, 0:256], basT["q"][:, c, :],
                               xn_fm[:, c, 0:256], start=(c == 0),
                               stop=(c == NCH - 1), skip_group_check=True)
            ins.then_inc(sm.h, 1)
        smrx = sm.inc(1)
        P.emit("tensor", f_resx)

        def f_resxe(e, smrx=smrx):
            e.wait_ge(sm.h, smrx)
            e.tensor_copy(res_qx[:], psE[0:32, 0:256]).then_inc(sv.h, 1)
        sv.inc(1)
        P.emit("vector", f_resxe)

        svf = sv.n

        def f_tf(e, svf=svf):
            e.wait_ge(sv.h, svf)
            e.transpose(psB[0:24, 0:128], fcols[:], id_f[:])
            e.transpose(psB[0:1, 128:256], fcol_b[:], id_f[:]).then_inc(sm.h, 1)
        smf = sm.inc(1)
        P.emit("tensor", f_tf)

        def f_ef(e, smf=smf):
            e.wait_ge(sm.h, smf)
            vstep(e, e.tensor_copy(frow16[:], psB[0:24, 0:128]))
            vstep(e, e.tensor_copy(rows[0:1, 0:128], psB[0:1, 128:256]))
            e.tensor_copy(f_row[0:1, bq_col:bq_col + 1],
                          rows[0:1, bq_part:bq_part + 1]).then_inc(sv.h, 1)
        svf2 = sv.inc(1)
        P.emit("vector", f_ef)

        b1 = dma_m.nxt()
        b2 = dma_m.nxt()

        b3 = dma_m.nxt()

        def f_gb(e, svf2=svf2, b1=b1, b2=b2, b3=b3):
            e.wait_ge(sv.h, svf2)
            e.dma_start(out=brow_f, in_=frow16[:]).then_inc(b1[0].h, 16)
            e.wait_ge(b1[0].h, b1[1])
            e.dma_start(out=f_row[0:1, 1:T + 1],
                        in_=brow_f[0:16].rearrange("a p -> (a p)").unsqueeze(0)
                        ).then_inc(b2[0].h, 16)
            e.dma_start(out=f_row[0:1, T + 2:T + 2 + EXTW],
                        in_=brow_f[16:22].rearrange("a p -> (a p)").unsqueeze(0)
                        ).then_inc(b3[0].h, 16)
        P.emit("gpsimd", f_gb)

        dma_m.pend(P, "gpsimd")
        st_w.pend(P, "gpsimd")
        st_z.pend(P, "gpsimd")
        P.pend("vector", sv)
        P.pend("tensor", sm)
        P.pend("scalar", sa)
        ld_x.pend(P, "sync")

    def p2():
        AR.reset(state["res_end"])
        res = state["res"]
        attn_h = AR.carve("attn_h", [128, NCH, TQ], BF16)
        state["attn_h"] = attn_h
        state["attnh_end"] = AR.off
        K_ext = AR.carve("K_ext", [65, SEQ], BF16)
        V_tm = AR.carve("V_tm", [128, HCH, 65], BF16)
        Q_ext = AR.carve("Q_ext", [65, TQ], BF16)
        probsT = AR.carve("probsT", [128, 2, 1024], BF16)
        rs64 = AR.carve("rs64", [128, 1024], F32)
        rsb = AR.carve("rsb", [128, 1024], F32)
        odd_sc = AR.carve("odd_sc", [64, 1024], BF16)
        mtmp = AR.carve("mtmp", [128, 8], F32)
        mcols = AR.carve("mcols", [128, 18], F32)
        mrow_bf = AR.carve("mrow_bf", [18, 128], BF16)

        def f_vinit(e):
            e.memset(K_ext[64:65, :], 1.0)
            e.memset(V_tm[:, :, 64:65], 1.0).then_inc(sv.h, 1)
        sv.inc(1)
        P.emit("vector", f_vinit)

        for h in range(p2_heads):
            hs = slice(h * 64, (h + 1) * 64)

            smg_prev = sm.n
            ev_ms = []
            for s in range(8):
                def f_mk(e, s=s, h=h, hs=hs, gv=sv.n):
                    if s == 0 and h == 0:
                        for (s_, v_) in state["ld_c_fin"]:
                            e.wait_ge(s_.h, v_)
                        e.wait_ge(ld_g.h, state["ld_g_total"])
                    if s == 0 and gv:
                        e.wait_ge(sv.h, gv)
                    e.matmul(psA[0:64, :], wT["k"][:, hs], res["k"][:, 512 * s:512 * (s + 1)],
                             start=True, stop=True,
                             skip_group_check=True).then_inc(sm.h, 1)
                smv = sm.inc(1)
                P.emit("tensor", f_mk)

                def f_ek(e, s=s, smv=smv):
                    e.wait_ge(sm.h, smv)
                    e.tensor_copy(K_ext[0:64, 512 * s:512 * (s + 1)],
                                  psA[0:64, :]).then_inc(sv.h, 1)
                ev_ms.append(sv.inc(1))
                P.emit("vector", f_ek)
                def f_wk(e, v=ev_ms[-1]):
                    e.wait_ge(sv.h, v)
                P.emit("tensor", f_wk)

            for s in range(5):
                def f_mq(e, s=s, h=h, hs=hs):
                    if s < 4:
                        e.matmul(psA[0:64, :], wT["q"][:, hs],
                                 res["q"][:, s0 + 512 * s:s0 + 512 * (s + 1)],
                                 start=True, stop=True,
                                 skip_group_check=True).then_inc(sm.h, 1)
                    else:
                        e.matmul(psA[0:64, 0:EXT], wT["q"][:, hs],
                                 state["res_qx"][:], start=True, stop=True,
                                 skip_group_check=True).then_inc(sm.h, 1)
                smv = sm.inc(1)
                P.emit("tensor", f_mq)

                def f_eq(e, s=s, smv=smv):
                    e.wait_ge(sm.h, smv)
                    if s < 4:
                        e.tensor_copy(Q_ext[0:64, 512 * s:512 * (s + 1)],
                                      psA[0:64, :]).then_inc(sv.h, 1)
                    else:
                        e.tensor_copy(Q_ext[0:64, T:TQ],
                                      psA[0:64, 0:EXT]).then_inc(sv.h, 1)
                v = sv.inc(1)
                P.emit("vector", f_eq)

                def f_wq(e, v=v):
                    e.wait_ge(sv.h, v)
                P.emit("tensor", f_wq)

            for kc in range(HCH):
                def f_mv(e, kc=kc, h=h, hs=hs):
                    e.matmul(psB[0:128, 0:64], res["v"][:, 128 * kc:128 * (kc + 1)],
                             wT["v"][:, hs], start=True, stop=True,
                             skip_group_check=True).then_inc(sm.h, 1)
                smv = sm.inc(1)
                P.emit("tensor", f_mv)

                def f_ev2(e, kc=kc, smv=smv):
                    e.wait_ge(sm.h, smv)
                    e.tensor_copy(V_tm[:, kc, 0:64], psB[0:128, 0:64]).then_inc(sv.h, 1)
                v = sv.inc(1)
                P.emit("vector", f_ev2)

                def f_wv(e, v=v):
                    e.wait_ge(sv.h, v)
                P.emit("tensor", f_wv)

            if p2_stage == "kqv":
                continue
            red_ms = {}
            for qt in range(18):
                for kc in range(8):
                    ps_s = psA if kc % 2 == 0 else psB
                    guard = red_ms.get((qt, kc - 2), red_ms.get((qt - 1, 6 + kc), 0))

                    def f_ms(e, qt=qt, kc=kc, ps_s=ps_s, guard=guard):
                        if guard:
                            e.wait_ge(sv.h, guard)
                        e.matmul(ps_s[:], Q_ext[0:64, 128 * qt:128 * (qt + 1)],
                                 K_ext[0:64, 512 * kc:512 * (kc + 1)],
                                 start=True, stop=True,
                                 skip_group_check=True).then_inc(sm.h, 1)
                    smv = sm.inc(1)
                    P.emit("tensor", f_ms)

                    def f_rs(e, kc=kc, ps_s=ps_s, smv=smv,
                             cmprev=red_ms.get((qt - 1, 9), 0)):
                        e.wait_ge(sm.h, smv)
                        if kc == 0 and cmprev:
                            e.wait_ge(sv.h, cmprev)
                        e.reduce_max(mtmp[:, kc:kc + 1], ps_s[:],
                                     axis=AX.X).then_inc(sv.h, 1)
                    red_ms[(qt, kc)] = sv.inc(1)
                    P.emit("vector", f_rs)

                def f_cm(e, qt=qt, v=red_ms[(qt, 7)]):
                    e.wait_ge(sv.h, v)
                    e.reduce_max(mcols[:, qt:qt + 1], mtmp[:], axis=AX.X,
                                 negate=True).then_inc(sv.h, 1)
                red_ms[(qt, 9)] = sv.inc(1)
                P.emit("vector", f_cm)

            if p2_stage == "spass":
                continue
            svm = sv.n

            def f_tm(e, svm=svm):
                e.wait_ge(sv.h, svm)
                e.transpose(psB[0:18, 0:128], mcols[:], id_f[:]).then_inc(sm.h, 1)
            smv = sm.inc(1)
            P.emit("tensor", f_tm)

            def f_em(e, smv=smv):
                e.wait_ge(sm.h, smv)
                e.tensor_copy(mrow_bf[:], psB[0:18, 0:128]).then_inc(sv.h, 1)
            svm2 = sv.inc(1)
            P.emit("vector", f_em)

            m1_ = dma_m.nxt()
            m2_ = dma_m.nxt()

            def f_gm(e, svm2=svm2, m1_=m1_, m2_=m2_):
                e.wait_ge(sv.h, svm2)
                e.dma_start(out=brow_m, in_=mrow_bf[:]).then_inc(m1_[0].h, 16)
                e.wait_ge(m1_[0].h, m1_[1])
                e.dma_start(out=Q_ext[64:65, :],
                            in_=brow_m.rearrange("a p -> (a p)").unsqueeze(0)
                            ).then_inc(m2_[0].h, 16)
            d1 = m2_
            P.emit("gpsimd", f_gm)

            if h % 2 == 1 and (h + 1) // 2 <= 7:
                ee = (h + 1) // 2

                def f_cast(e, ee=ee):
                    e.dma_start(out=gate_bf[ee],
                                in_=wgate_d[ee]).then_inc(cast_s[ee].h, 16)
                    e.dma_start(out=up_bf[ee],
                                in_=wup_d[ee]).then_inc(cast_s[ee].h, 16)
                    e.dma_start(out=down_bf[ee],
                                in_=wdown_d[ee]).then_inc(cast_s[ee].h, 16)
                cast_s[ee].inc(48)
                P.emit("gpsimd", f_cast)

            if p2_stage == "mrow":
                continue
            segs = [(0, 0, 1024), (1, 1024, 1024), (2, 2048, EXT)]
            for (hf, q_off, qw) in segs:
                exp_ms = {}
                av_last = 0
                for kc in range(p2_kc):
                    ps_st = psC if kc % 2 == 0 else psD
                    guard = exp_ms.get(kc - 2, 0)

                    def f_st2(e, kc=kc, ps_st=ps_st, guard=guard, d1=d1, hf=hf,
                              q_off=q_off, qw=qw, gv=sv.n, ga=sa.n):
                        if kc == 0 and hf == 0:
                            e.wait_ge(d1[0].h, d1[1])
                        if guard:
                            e.wait_ge(sa.h, guard)
                        if kc < 2:
                            if gv:
                                e.wait_ge(sv.h, gv)
                            if ga:
                                e.wait_ge(sa.h, ga)
                        for (so, sw) in ([(0, 512), (512, 512)] if qw == 1024
                                         else [(0, qw)]):
                            ins = e.matmul(
                                ps_st[:, so:so + sw],
                                K_ext[:, 128 * kc:128 * (kc + 1)],
                                Q_ext[:, q_off + so:q_off + so + sw],
                                start=True, stop=True, skip_group_check=True)
                        ins.then_inc(sm.h, 1)
                    sm.inc(1)
                    smv = sm.n
                    P.emit("tensor", f_st2)

                    if p2_stage == "st0":
                        continue
                    if p2_stage == "stf32" and kc > 0:
                        pass

                    def f_ex(e, kc=kc, ps_st=ps_st, smv=smv):
                        e.wait_ge(sm.h, smv)
                        if p2_stage == "stf32":
                            e.activation(rs64[:], ps_st[:], AF.Exp).then_inc(sa.h, 1)
                        else:
                            e.activation(probsT[:, kc % 2, :], ps_st[:],
                                         AF.Exp).then_inc(sa.h, 1)
                    exp_ms[kc] = sa.inc(1)
                    P.emit("scalar", f_ex)

                    if p2_stage == "st":
                        continue

                    def f_av(e, kc=kc, v=exp_ms[kc]):
                        e.wait_ge(sa.h, v)
                        e.matmul(psE[:, 0:512], V_tm[:, kc, :], probsT[:, kc % 2, 0:512],
                                 start=(kc == 0), stop=(kc == p2_kc - 1),
                                 skip_group_check=True)
                        e.matmul(psE[:, 512:1024], V_tm[:, kc, :],
                                 probsT[:, kc % 2, 512:1024],
                                 start=(kc == 0), stop=(kc == p2_kc - 1),
                                 skip_group_check=True).then_inc(sm.h, 1)
                    sm.inc(1)
                    av_last = sm.n
                    P.emit("tensor", f_av)

                if p2_stage in ("st", "st0", "stf32", "av"):
                    continue
                def f_rc(e, v=av_last, gd=getattr(p2, "_last_rs", None), qw=qw):
                    e.wait_ge(sm.h, v)
                    if gd:
                        e.wait_ge(gd[0].h, gd[1])
                    e.reciprocal(rs64[64:65, 0:qw], psE[64:65, 0:qw]).then_inc(sv.h, 1)
                svr = sv.inc(1)
                P.emit("vector", f_rc)

                if p2_stage == "norm1":
                    continue
                d2 = dma_rs.nxt()
                p2._last_rs = d2

                def f_gs(e, svr=svr, d2=d2, qw=qw):
                    e.wait_ge(sv.h, svr)
                    e.dma_start(out=rs64[0:1, 0:qw],
                                in_=rs64[64:65, 0:qw]).then_inc(d2[0].h, 16)
                P.emit("gpsimd", f_gs)

                def f_bc(e, d2=d2, qw=qw):
                    e.wait_ge(d2[0].h, d2[1])
                    for (so, sw) in ([(0, 512), (512, 512)] if qw == 1024
                                     else [(0, qw)]):
                        ins = e.matmul(psC[:, so:so + sw], ones_f[:],
                                       rs64[0:1, so:so + sw],
                                       start=True, stop=True, skip_group_check=True)
                    ins.then_inc(sm.h, 1)
                smb = sm.inc(1)
                P.emit("tensor", f_bc)

                if p2_stage == "norm2":
                    continue
                c_chunk = h // 2
                base64 = (h % 2) * 64

                def f_nm(e, smb=smb, hf=hf, h=h, c_chunk=c_chunk, base64=base64,
                         gd=getattr(p2, "_last_od", None), q_off=q_off, qw=qw):
                    e.wait_ge(sm.h, smb)
                    if base64 == 64 and gd:
                        e.wait_ge(gd[0].h, gd[1])
                    vstep(e, e.tensor_copy(rsb[:, 0:qw], psC[:, 0:qw]))
                    e.nop().then_inc(sv.h, 1)
                    dst = (attn_h[0:64, c_chunk, q_off:q_off + qw]
                           if base64 == 0 else odd_sc[:, 0:qw])
                    e.tensor_tensor(dst, psE[0:64, 0:qw], rsb[0:64, 0:qw],
                                    ALU.mult).then_inc(sv.h, 1)
                sv.inc(2)
                svn = sv.n
                P.emit("vector", f_nm)

                if base64 == 64:
                    d3 = dma_od.nxt()
                    p2._last_od = d3

                    def f_go(e, svn=svn, hf=hf, c_chunk=c_chunk, d3=d3,
                             q_off=q_off, qw=qw):
                        e.wait_ge(sv.h, svn)
                        e.dma_start(out=attn_h[64:128, c_chunk, q_off:q_off + qw],
                                    in_=odd_sc[:, 0:qw]).then_inc(d3[0].h, 16)
                    P.emit("gpsimd", f_go)

                    def f_wo(e, d3=d3):
                        e.wait_ge(d3[0].h, d3[1])
                    P.emit("tensor", f_wo)
                else:
                    def f_wo(e, svn=svn):
                        e.wait_ge(sv.h, svn)
                    P.emit("tensor", f_wo)

        dma_rs.pend(P, "gpsimd")
        dma_od.pend(P, "gpsimd")
        dma_m.pend(P, "gpsimd")
        P.pend("vector", sv)
        P.pend("scalar", sa)
        P.pend("tensor", sm)

    def p3():
        attn_h = state["attn_h"]
        AR.reset(state["attnh_end"])
        res_ao = AR.carve("res_ao", [32, TQ], BF16)

        lxf = ld_x.nxt()

        def f_lxf(e, lxf=lxf):
            e.dma_start(out=x_land[:], in_=x_extra_fm).then_inc(lxf[0].h, 16)
        P.emit("sync", f_lxf)

        for (t0, tw) in [(0, 512), (512, 512), (1024, 512), (1536, 512),
                         (2048, EXT)]:
            svt = sv.n

            def f_s1(e, t0=t0, tw=tw, svt=svt):
                e.wait_ge(sv.h, svt)
                for c in range(NCH):
                    ins = e.matmul(psE[0:32, 0:tw], basT["o"][:, c, :],
                                   attn_h[:, c, t0:t0 + tw],
                                   start=(c == 0), stop=(c == NCH - 1),
                                   skip_group_check=True)
                ins.then_inc(sm.h, 1)
            smv = sm.inc(1)
            P.emit("tensor", f_s1)

            def f_e1(e, t0=t0, tw=tw, smv=smv):
                e.wait_ge(sm.h, smv)
                e.tensor_copy(res_ao[:, t0:t0 + tw],
                              psE[0:32, 0:tw]).then_inc(sv.h, 1)
            sve = sv.inc(1)
            P.emit("vector", f_e1)

            for oc in range(NCH):
                ps_s = psA if oc % 2 == 0 else psB

                def f_s2(e, t0=t0, tw=tw, oc=oc, ps_s=ps_s, guard=sv.n):
                    e.wait_ge(sv.h, guard)
                    e.matmul(ps_s[:, 0:tw], wT["o"][:, 128 * oc:128 * (oc + 1)],
                             res_ao[:, t0:t0 + tw],
                             start=True, stop=True,
                             skip_group_check=True).then_inc(sm.h, 1)
                smv2 = sm.inc(1)
                P.emit("tensor", f_s2)

                def f_a2(e, t0=t0, tw=tw, oc=oc, ps_s=ps_s, smv2=smv2, lxf=lxf):
                    e.wait_ge(sm.h, smv2)
                    if t0 < T:
                        e.tensor_tensor(x_res[:, oc, t0:t0 + tw],
                                        ps_s[:, 0:tw], x_res[:, oc, t0:t0 + tw],
                                        ALU.add).then_inc(sv.h, 1)
                    else:
                        e.wait_ge(lxf[0].h, lxf[1])
                        e.tensor_tensor(x_land[:, oc, :],
                                        ps_s[:, 0:tw], x_land[:, oc, :],
                                        ALU.add).then_inc(sv.h, 1)
                sv.inc(1)
                P.emit("vector", f_a2)

        P.pend("vector", sv)
        P.pend("tensor", sm)
        ld_x.pend(P, "sync")

    def p4():
        AR.reset()
        xq_t = AR.carve("xq_t", [128, 2, NCH, 130], BF16)
        xq_x = AR.carve("xq_x", [128, NCH, EXTW], BF16)
        w1sb = AR.carve("w1sb", [128, 3, NCH, 128], BF16)
        w2sb = AR.carve("w2sb", [128, 3, D], BF16)
        st4 = AR.carve("st4", [128, 4], F32)
        row4 = AR.carve("row4", [1, 256], F32)
        ov2 = AR.off
        h_sb = AR.carve("h_sb", [128, HCH, 130], F32)
        y_sb = AR.carve("y_sb", [128, HCH, 128], F32)
        y2_sb = AR.carve("y2_sb", [128, HCH, 128], F32)
        t2_sb = AR.carve("t2_sb", [128, HCH, 128], BF16)
        fbc_sb = AR.carve("fbc_sb", [128, 130], F32)
        sbc_sb = AR.carve("sbc_sb", [128, 128], F32)
        f2bc_sb = AR.carve("f2bc_sb", [128, 128], F32)
        state["h_sb"] = h_sb
        state["t2_sb"] = t2_sb
        state["y2_sb"] = y2_sb
        state["sbc"] = sbc_sb
        state["f2bc"] = f2bc_sb

        t0b = convw[:, :, 0:1].broadcast_to([128, HCH, 128])
        t1b = convw[:, :, 1:2].broadcast_to([128, HCH, 128])
        t2b = convw[:, :, 2:3].broadcast_to([128, HCH, 128])
        cbb = convb[:, :, 0:1].broadcast_to([128, HCH, 128])

        w1_ms = {}
        w2_ms = {}

        for i in range(16):
            ecol = slice(128 * i, 128 * i + 130)

            lq = ld_q.nxt()
            tile_done_g = getattr(p4, "_tile_done", {}).get(i - 2, 0)

            def f_lxq(e, i=i, lq=lq, g=tile_done_g):
                if g:
                    e.wait_ge(sm.h, g)
                e.dma_start(out=xq_t[:, i % 2],
                            in_=xq_hbm[:, :, 128 * i:128 * i + 130]
                            ).then_inc(lq[0].h, 16)
            P.emit("sync", f_lxq)

            svg = sv.n

            def f_fb(e, i=i, svg=svg):
                e.wait_ge(sv.h, svg)
                e.matmul(psD[:, 0:130], ones_f[:], f_row[0:1, 128 * i:128 * i + 130],
                         start=True, stop=True, skip_group_check=True).then_inc(sm.h, 1)
            smf = sm.inc(1)
            P.emit("tensor", f_fb)

            def f_fbe(e, smf=smf):
                e.wait_ge(sm.h, smf)
                e.tensor_copy(fbc_sb[:], psD[:, 0:130]).then_inc(sv.h, 1)
            svfb = sv.inc(1)
            P.emit("vector", f_fbe)

            for hc in range(HCH):
                buf = hc % 3
                lw = ld_s.nxt()
                if hc >= 3:
                    guard = w1_ms.get((i, hc - 3), 0)
                else:
                    prev_hc = max(x for x in range(HCH) if x % 3 == hc % 3)
                    guard = w1_ms.get((i - 1, prev_hc), 0)

                def f_lw(e, i=i, hc=hc, buf=buf, lw=lw, guard=guard):
                    if guard:
                        e.wait_ge(sm.h, guard)
                    e.dma_start(out=w1sb[:, buf],
                                in_=w1q_hbm[hc].rearrange("ic p q -> p ic q")
                                ).then_inc(lw[0].h, 16)
                P.emit("sync", f_lw)

                ps_h = psA if hc % 2 == 0 else psB
                hev_guard = getattr(p4, "_hev", {}).get((i, hc - 2),
                             getattr(p4, "_hev", {}).get((i - 1, HCH - 2 + hc), 0))

                def f_mh(e, i=i, hc=hc, buf=buf, lw=lw, ps_h=ps_h,
                         hev_guard=hev_guard, lq=lq):
                    e.wait_ge(lw[0].h, lw[1])
                    e.wait_ge(lq[0].h, lq[1])
                    if hev_guard:
                        e.wait_ge(sv.h, hev_guard)
                    for ic in range(NCH):
                        ins = e.matmul(ps_h[:, 0:130], w1sb[:, buf, ic, :],
                                       xq_t[:, i % 2, ic, :],
                                       start=(ic == 0), stop=(ic == NCH - 1),
                                       skip_group_check=True)
                    ins.then_inc(sm.h, 1)
                w1_ms[(i, hc)] = sm.inc(1)
                P.emit("tensor", f_mh)

                def f_eh(e, i=i, hc=hc, ps_h=ps_h, smv=w1_ms[(i, hc)], svfb=svfb):
                    e.wait_ge(sm.h, smv)
                    e.wait_ge(sv.h, svfb)
                    e.tensor_tensor(h_sb[:, hc, :], ps_h[:, 0:130], fbc_sb[:],
                                    ALU.mult).then_inc(sv.h, 1)
                if not hasattr(p4, "_hev"):
                    p4._hev = {}
                p4._hev[(i, hc)] = sv.inc(1)
                P.emit("vector", f_eh)

            h_end = p4._hev[(i, HCH - 1)]

            def f_cv(e, i=i, h_end=h_end):
                e.wait_ge(sv.h, h_end)
                vstep(e, e.tensor_tensor(y_sb[:], h_sb[:, :, 0:128], t0b, ALU.mult))
                vstep(e, e.tensor_tensor(y2_sb[:], h_sb[:, :, 1:129], t1b, ALU.mult))
                vstep(e, e.tensor_tensor(y_sb[:], y_sb[:], y2_sb[:], ALU.add))
                vstep(e, e.tensor_tensor(y2_sb[:], h_sb[:, :, 2:130], t2b, ALU.mult))
                vstep(e, e.tensor_tensor(y_sb[:], y_sb[:], y2_sb[:], ALU.add))
                e.tensor_tensor(y_sb[:], y_sb[:], cbb, ALU.add).then_inc(sv.h, 1)
            svc = sv.inc(1)
            P.emit("vector", f_cv)

            def f_si(e, svc=svc):
                e.wait_ge(sv.h, svc)
                e.activation(y2_sb[:], y_sb[:], AF.Silu).then_inc(sa.h, 1)
            sas = sa.inc(1)
            P.emit("scalar", f_si)

            def f_ab(e, sas=sas):
                e.wait_ge(sa.h, sas)
                e.activation(y_sb[:], y2_sb[:], AF.Abs).then_inc(sa.h, 1)
            sab = sa.inc(1)
            P.emit("scalar", f_ab)

            def f_q1(e, sab=sab):
                e.wait_ge(sa.h, sab)
                vstep(e, e.tensor_tensor(y_sb[:, 0:16, :], y_sb[:, 0:16, :],
                                         y_sb[:, 16:32, :], ALU.max))
                vstep(e, e.tensor_tensor(y_sb[:, 0:8, :], y_sb[:, 0:8, :],
                                         y_sb[:, 8:16, :], ALU.max))
                vstep(e, e.tensor_tensor(y_sb[:, 0:4, :], y_sb[:, 0:4, :],
                                         y_sb[:, 4:8, :], ALU.max))
                vstep(e, e.tensor_tensor(y_sb[:, 0:2, :], y_sb[:, 0:2, :],
                                         y_sb[:, 2:4, :], ALU.max))
                e.tensor_tensor(y_sb[:, 0:1, :], y_sb[:, 0:1, :],
                                y_sb[:, 1:2, :], ALU.max).then_inc(sv.h, 1)
            svq = sv.inc(1)
            P.emit("vector", f_q1)

            def f_tq(e, svq=svq):
                e.wait_ge(sv.h, svq)
                e.transpose(psD[:, 256:384], y_sb[:, 0, :], id_f[:]).then_inc(sm.h, 1)
            smq = sm.inc(1)
            P.emit("tensor", f_tq)

            def f_q2(e, smq=smq):
                e.wait_ge(sm.h, smq)
                vstep(e, e.reduce_max(st4[:, 0:1], psD[:, 256:384], axis=AX.X))
                e.tensor_scalar(st4[:, 1:2], st4[:, 0:1], 1e-5, None,
                                ALU.max).then_inc(sv.h, 1)
            svq2 = sv.inc(1)
            P.emit("vector", f_q2)

            def f_tq2(e, svq2=svq2):
                e.wait_ge(sv.h, svq2)
                e.transpose(psD[0:1, 384:512], st4[:, 1:2], id_f[:]).then_inc(sm.h, 1)
            smq2 = sm.inc(1)
            P.emit("tensor", f_tq2)

            def f_q3(e, smq2=smq2):
                e.wait_ge(sm.h, smq2)
                vstep(e, e.tensor_copy(row4[0:1, 0:128], psD[0:1, 384:512]))
                vstep(e, e.reciprocal(row4[0:1, 128:256], row4[0:1, 0:128]))
                vstep(e, e.tensor_scalar(row4[0:1, 128:256], row4[0:1, 128:256],
                                         127.0, None, ALU.mult))
                e.tensor_scalar(row4[0:1, 0:128], row4[0:1, 0:128],
                                mc2[0:1, 0:1], None, ALU.mult).then_inc(sv.h, 1)
            svq3 = sv.inc(1)
            P.emit("vector", f_q3)

            def f_bq(e, svq3=svq3):
                e.wait_ge(sv.h, svq3)
                e.matmul(psD[:, 512:640], ones_f[:], row4[0:1, 128:256],
                         start=True, stop=True, skip_group_check=True)
                e.matmul(psD[:, 640:768], ones_f[:], row4[0:1, 0:128],
                         start=True, stop=True, skip_group_check=True).then_inc(sm.h, 1)
            smb = sm.inc(1)
            P.emit("tensor", f_bq)

            def f_q4(e, smb=smb):
                e.wait_ge(sm.h, smb)
                vstep(e, e.tensor_copy(sbc_sb[:], psD[:, 512:640]))
                vstep(e, e.tensor_copy(f2bc_sb[:], psD[:, 640:768]))
                vstep(e, e.tensor_tensor(y_sb[:], y2_sb[:],
                                         sbc_sb[:].unsqueeze(1).broadcast_to([128, HCH, 128]),
                                         ALU.mult))
                vstep(e, e.tensor_scalar(y_sb[:], y_sb[:], MAGIC, MAGIC,
                                         ALU.add, ALU.subtract))
                e.tensor_scalar(t2_sb[:], y_sb[:], 127.0, -128.0,
                                ALU.min, ALU.max).then_inc(sv.h, 1)
            svt2 = sv.inc(1)
            P.emit("vector", f_q4)

            for kc in range(HCH):
                buf = kc % 3
                lw2 = ld_s2.nxt()
                if kc >= 3:
                    guard = w2_ms.get((i, kc - 3), 0)
                else:
                    prev_kc = max(x for x in range(HCH) if x % 3 == kc % 3)
                    guard = w2_ms.get((i - 1, prev_kc), 0)

                def f_lw2(e, kc=kc, buf=buf, lw2=lw2, guard=guard):
                    if guard:
                        e.wait_ge(sm.h, guard)
                    e.dma_start(out=w2sb[:, buf], in_=w2q_hbm[kc]).then_inc(lw2[0].h, 16)
                P.emit("sync", f_lw2)

                def f_m2(e, i=i, kc=kc, buf=buf, lw2=lw2, svt2=svt2, gev=sv.n):
                    e.wait_ge(lw2[0].h, lw2[1])
                    if kc == 0:
                        e.wait_ge(sv.h, svt2)
                        e.wait_ge(sv.h, gev)
                    for oc in range(NCH):
                        ins = e.matmul(psC[:].rearrange("p (a q) -> p a q", q=128)[:, oc, :],
                                       w2sb[:, buf, 128 * oc:128 * (oc + 1)],
                                       t2_sb[:, kc, :],
                                       start=(kc == 0 and oc % 4 == 0),
                                       stop=(kc == HCH - 1),
                                       skip_group_check=True)
                    ins.then_inc(sm.h, 1)
                w2_ms[(i, kc)] = sm.inc(1)
                P.emit("tensor", f_m2)

            if not hasattr(p4, "_tile_done"):
                p4._tile_done = {}
            p4._tile_done[i] = w2_ms[(i, HCH - 1)]

            def f_e2(e, i=i, smv=w2_ms[(i, HCH - 1)]):
                e.wait_ge(sm.h, smv)
                vstep(e, e.tensor_tensor(
                    y_sb[:, 0:8, :],
                    psC[:].rearrange("p (a q) -> p a q", q=128),
                    f2bc_sb[:].unsqueeze(1).broadcast_to([128, NCH, 128]),
                    ALU.mult))
                e.tensor_tensor(x_res[:, :, 128 * i:128 * (i + 1)],
                                x_res[:, :, 128 * i:128 * (i + 1)],
                                y_sb[:, 0:8, :], ALU.add).then_inc(sv.h, 1)
            sv.inc(1)
            P.emit("vector", f_e2)

        AR.reset(ov2)
        silu_x = AR.carve("silu_x", [128, HCH, EXT], F32)
        t2x = AR.carve("t2x", [128, HCH, EXT], BF16)
        hx = AR.carve("hx", [128, EXTW], F32)
        yx = AR.carve("yx", [128, EXT], F32)
        fxb = AR.carve("fxb", [128, EXTW], F32)
        sxb = AR.carve("sxb", [128, EXT], F32)
        f2xb = AR.carve("f2xb", [128, EXT], F32)
        rmax = AR.carve("rmax", [128, EXT], F32)
        rmin = AR.carve("rmin", [128, EXT], F32)

        lqx = ld_q.nxt()

        def f_lqx(e, lqx=lqx, g=p4._tile_done[15]):
            e.wait_ge(sm.h, g)
            e.dma_start(out=xq_x[:],
                        in_=xq_hbm[:, :, T + 2:T + 2 + EXTW]).then_inc(lqx[0].h, 16)
        P.emit("sync", f_lqx)

        svg2 = sv.n

        def f_fbx(e, svg2=svg2):
            e.wait_ge(sv.h, svg2)
            e.matmul(psD[:, 0:512], ones_f[:], f_row[0:1, T + 2:T + 2 + 512],
                     start=True, stop=True, skip_group_check=True)
            e.matmul(psD[:, 512:768], ones_f[:],
                     f_row[0:1, T + 2 + 512:T + 2 + EXTW],
                     start=True, stop=True, skip_group_check=True).then_inc(sm.h, 1)
        smfx = sm.inc(1)
        P.emit("tensor", f_fbx)

        def f_fbxe(e, smfx=smfx):
            e.wait_ge(sm.h, smfx)
            e.tensor_copy(fxb[:], psD[:, 0:768]).then_inc(sv.h, 1)
        svfx = sv.inc(1)
        P.emit("vector", f_fbxe)

        hx_ms = {}
        for hc in range(HCH):
            buf = hc % 3
            lw = ld_s.nxt()
            if hc >= 3:
                guard = hx_ms.get(hc - 3, 0)
            else:
                guard = w1_ms.get((15, max(x for x in range(HCH)
                                           if x % 3 == hc % 3)), 0)

            def f_lwx(e, hc=hc, buf=buf, lw=lw, guard=guard):
                if guard:
                    e.wait_ge(sm.h, guard)
                e.dma_start(out=w1sb[:, buf],
                            in_=w1q_hbm[hc].rearrange("ic p q -> p ic q")
                            ).then_inc(lw[0].h, 16)
            P.emit("sync", f_lwx)

            def f_mhx(e, hc=hc, buf=buf, lw=lw, lqx=lqx, gsv=sv.n):
                e.wait_ge(lw[0].h, lw[1])
                e.wait_ge(lqx[0].h, lqx[1])
                e.wait_ge(sv.h, gsv)
                for (so, sw) in [(0, 512), (512, 256)]:
                    for ic in range(NCH):
                        ins = e.matmul(psC[:, so:so + sw], w1sb[:, buf, ic, :],
                                       xq_x[:, ic, so:so + sw],
                                       start=(ic == 0), stop=(ic == NCH - 1),
                                       skip_group_check=True)
                ins.then_inc(sm.h, 1)
            hx_ms[hc] = sm.inc(1)
            P.emit("tensor", f_mhx)

            def f_cvx(e, hc=hc, smv=hx_ms[hc], svfx=svfx):
                e.wait_ge(sm.h, smv)
                e.wait_ge(sv.h, svfx)
                vstep(e, e.tensor_tensor(hx[:], psC[:, 0:768], fxb[:], ALU.mult))
                vstep(e, e.tensor_scalar(yx[:], hx[:, 256:512],
                                         convw[:, hc, 0:1], None, ALU.mult))
                vstep(e, e.scalar_tensor_tensor(yx[:], hx[:, 0:256],
                                                convw[:, hc, 1:2], yx[:],
                                                ALU.mult, ALU.add))
                vstep(e, e.scalar_tensor_tensor(yx[:], hx[:, 512:768],
                                                convw[:, hc, 2:3], yx[:],
                                                ALU.mult, ALU.add))
                e.tensor_scalar(yx[:], yx[:], convb[:, hc, 0:1], None,
                                ALU.add).then_inc(sv.h, 1)
            svcx = sv.inc(1)
            P.emit("vector", f_cvx)

            def f_six(e, hc=hc, svcx=svcx):
                e.wait_ge(sv.h, svcx)
                e.activation(silu_x[:, hc, :], yx[:], AF.Silu).then_inc(sa.h, 1)
            sasix = sa.inc(1)
            P.emit("scalar", f_six)

            def f_rmx(e, hc=hc, sasix=sasix):
                e.wait_ge(sa.h, sasix)
                if hc == 0:
                    vstep(e, e.tensor_copy(rmax[:], silu_x[:, 0, :]))
                    e.tensor_copy(rmin[:], silu_x[:, 0, :]).then_inc(sv.h, 1)
                else:
                    vstep(e, e.tensor_tensor(rmax[:], rmax[:], silu_x[:, hc, :],
                                             ALU.max))
                    e.tensor_tensor(rmin[:], rmin[:], silu_x[:, hc, :],
                                    ALU.min).then_inc(sv.h, 1)
            sv.inc(1)
            P.emit("vector", f_rmx)

        def f_q1x(e, gv=sv.n):
            e.wait_ge(sv.h, gv)
            vstep(e, e.tensor_scalar(rmin[:], rmin[:], -1.0, None, ALU.mult))
            e.tensor_tensor(rmax[:], rmax[:], rmin[:], ALU.max).then_inc(sv.h, 1)
        svm1 = sv.inc(1)
        P.emit("vector", f_q1x)

        def f_tqx(e, svm1=svm1):
            e.wait_ge(sv.h, svm1)
            e.transpose(psD[:, 0:128], rmax[:, 0:128], id_f[:])
            e.transpose(psD[:, 128:256], rmax[:, 128:256],
                        id_f[:]).then_inc(sm.h, 1)
        smq1 = sm.inc(1)
        P.emit("tensor", f_tqx)

        def f_q2x(e, smq1=smq1):
            e.wait_ge(sm.h, smq1)
            vstep(e, e.reduce_max(st4[:, 0:1], psD[:, 0:128], axis=AX.X))
            vstep(e, e.reduce_max(st4[:, 1:2], psD[:, 128:256], axis=AX.X))
            e.tensor_scalar(st4[:, 0:2], st4[:, 0:2], 1e-5, None,
                            ALU.max).then_inc(sv.h, 1)
        svq2x = sv.inc(1)
        P.emit("vector", f_q2x)

        def f_tq2x(e, svq2x=svq2x):
            e.wait_ge(sv.h, svq2x)
            e.transpose(psD[0:1, 256:384], st4[:, 0:1], id_f[:])
            e.transpose(psD[0:1, 384:512], st4[:, 1:2], id_f[:]).then_inc(sm.h, 1)
        smq2x = sm.inc(1)
        P.emit("tensor", f_tq2x)

        def f_q3x(e, smq2x=smq2x):
            e.wait_ge(sm.h, smq2x)
            vstep(e, e.tensor_copy(row4[0:1, 0:128], psD[0:1, 256:384]))
            vstep(e, e.tensor_copy(row4[0:1, 128:256], psD[0:1, 384:512]))
            vstep(e, e.reciprocal(rows[0:1, 0:256], row4[0:1, 0:256]))
            vstep(e, e.tensor_scalar(rows[0:1, 0:256], rows[0:1, 0:256],
                                     127.0, None, ALU.mult))
            e.tensor_scalar(row4[0:1, 0:256], row4[0:1, 0:256],
                            mc2[0:1, 0:1], None, ALU.mult).then_inc(sv.h, 1)
        svq3x = sv.inc(1)
        P.emit("vector", f_q3x)

        def f_bqx(e, svq3x=svq3x):
            e.wait_ge(sv.h, svq3x)
            e.matmul(psD[:, 0:256], ones_f[:], rows[0:1, 0:256],
                     start=True, stop=True, skip_group_check=True)
            e.matmul(psD[:, 256:512], ones_f[:], row4[0:1, 0:256],
                     start=True, stop=True, skip_group_check=True).then_inc(sm.h, 1)
        smbx = sm.inc(1)
        P.emit("tensor", f_bqx)

        def f_q4x(e, smbx=smbx):
            e.wait_ge(sm.h, smbx)
            vstep(e, e.tensor_copy(sxb[:], psD[:, 0:256]))
            vstep(e, e.tensor_copy(f2xb[:], psD[:, 256:512]))
            vstep(e, e.tensor_tensor(
                silu_x[:], silu_x[:],
                sxb[:].unsqueeze(1).broadcast_to([128, HCH, EXT]), ALU.mult))
            vstep(e, e.tensor_scalar(silu_x[:], silu_x[:], MAGIC, MAGIC,
                                     ALU.add, ALU.subtract))
            e.tensor_scalar(t2x[:], silu_x[:], 127.0, -128.0, ALU.min,
                            ALU.max).then_inc(sv.h, 1)
        svt2x = sv.inc(1)
        P.emit("vector", f_q4x)

        w2x_ms = {}
        for kc in range(HCH):
            buf = kc % 3
            lw2 = ld_s2.nxt()
            if kc >= 3:
                guard = w2x_ms.get(kc - 3, 0)
            else:
                guard = w2_ms.get((15, max(x for x in range(HCH)
                                           if x % 3 == kc % 3)), 0)

            def f_lw2x(e, kc=kc, buf=buf, lw2=lw2, guard=guard):
                if guard:
                    e.wait_ge(sm.h, guard)
                e.dma_start(out=w2sb[:, buf], in_=w2q_hbm[kc]).then_inc(lw2[0].h, 16)
            P.emit("sync", f_lw2x)

            def f_m2x(e, kc=kc, buf=buf, lw2=lw2, svt2x=svt2x, gev=sv.n):
                e.wait_ge(lw2[0].h, lw2[1])
                if kc == 0:
                    e.wait_ge(sv.h, svt2x)
                    e.wait_ge(sv.h, gev)
                for oc in range(NCH):
                    ins = e.matmul(psAll[:, 256 * oc:256 * (oc + 1)],
                                   w2sb[:, buf, 128 * oc:128 * (oc + 1)],
                                   t2x[:, kc, :],
                                   start=(kc == 0 and oc % 2 == 0),
                                   stop=(kc == HCH - 1),
                                   skip_group_check=True)
                ins.then_inc(sm.h, 1)
            w2x_ms[kc] = sm.inc(1)
            P.emit("tensor", f_m2x)

        def f_e2x(e, smv=w2x_ms[HCH - 1]):
            e.wait_ge(sm.h, smv)
            vstep(e, e.tensor_tensor(
                silu_x[:, 0:8, :],
                psAll[:, 0:2048].rearrange("p (a q) -> p a q", q=256),
                f2xb[:].unsqueeze(1).broadcast_to([128, NCH, EXT]), ALU.mult))
            e.tensor_tensor(x_land[:], x_land[:], silu_x[:, 0:8, :],
                            ALU.add).then_inc(sv.h, 1)
        sv.inc(1)
        P.emit("vector", f_e2x)

        ld_s.pend(P, "sync")
        ld_s2.pend(P, "sync")
        P.pend("vector", sv)
        P.pend("tensor", sm)
        P.pend("scalar", sa)


    def q0():
        cl = [(selb, selb_d), (bc2, bc2_d), (bq_sb, bq_d), (bk_sb, bk_d),
              (bo_sb, bo_d)]
        pairs = [(dst, srcd, ld_c2.nxt()) for (dst, srcd) in cl]

        def f_sy(e, pairs=pairs):
            for (dst, srcd, (s_, v_)) in pairs:
                e.dma_start(out=dst[:], in_=srcd).then_inc(s_.h, 16)
        P.emit("sync", f_sy)

        def f_g(e):
            for (dst, srcd) in [
                (selq, selq_d), (wct, wct_d), (wcb, wcb_d),
                (WeT1, WeT_d[0:128]), (WeT2, WeT_d[128:200]),
                (WrT1, WrT_d[0:128]), (WrT2, WrT_d[128:200]),
                (wr8, wrt_d.rearrange("(c p) e -> p c e", p=128)),
                (wg8, wgt_d.rearrange("(c p) e -> p c e", p=128)),
            ]:
                e.dma_start(out=dst[:], in_=srcd).then_inc(ld_g2.h, 16)
            e.dma_start(out=wqkv_bf[0], in_=wq_d).then_inc(cast_w.h, 16)
            e.dma_start(out=wqkv_bf[1], in_=wk_d).then_inc(cast_w.h, 16)
            e.dma_start(out=wqkv_bf[2], in_=wo_d).then_inc(cast_w.h, 16)
            for ee in range(1):
                e.dma_start(out=gate_bf[ee], in_=wgate_d[ee]).then_inc(cast_s[ee].h, 16)
                e.dma_start(out=up_bf[ee], in_=wup_d[ee]).then_inc(cast_s[ee].h, 16)
                e.dma_start(out=down_bf[ee], in_=wdown_d[ee]).then_inc(cast_s[ee].h, 16)
        P.emit("gpsimd", f_g)
        ld_g2.inc(16 * 9)
        cast_w.inc(48)
        cast_s[0].inc(48)
        state["consts_l1"] = [(s, s.n) for s in ld_c2.sems if s.n > 0] \
            + [(ld_g2, ld_g2.n)]

    def q1():
        psC = psAll[:, 1024:1536]
        psD = psAll[:, 1536:2048]
        AR.reset()
        x1n = AR.carve("x1n", [128, NCH, T], BF16)
        state["x1n"] = x1n
        ovq = AR.off
        xlr1 = AR.carve("xlr1", [128, D], BF16)
        xlr2 = AR.carve("xlr2", [128, D], BF16)
        state["q1_keep"] = AR.off
        ovx = AR.off
        xt1 = AR.carve("xt1", [128, D], F32)
        xt2 = AR.carve("xt2", [128, D], F32)
        sq_l = AR.carve("sq_l", [128, D], F32)
        ovx_end = AR.off
        sqb = AR.carve("sqb", [128, NCH, 512], BF16)
        x1bf = AR.carve("x1bf", [128, NCH, 512], BF16)
        ab_bc = AR.carve("ab_bc", [128, 512], F32)
        bb_bc = AR.carve("bb_bc", [128, 512], F32)
        stt = AR.carve("stt", [128, 8], F32)
        rg_tm = AR.carve("rg_tm", [128, 2, 4, NE], F32)
        wesb = AR.carve("wesb", [20, 2, 64], BF16)
        wfsb = AR.carve("wfsb", [128, 2, 64], BF16)

        consts = state["consts_l1"]

        svtl = sv.n

        def f_tl(e, svtl=svtl):
            e.wait_ge(sv.h, svtl)
            for c in range(NCH):
                ins = e.transpose(psAll[:, 1024 + 128 * c:1024 + 128 * (c + 1)],
                                  x_land[:, c, 0:128], id_f[:])
            ins.then_inc(sm.h, 1)
            for c in range(NCH):
                ins = e.transpose(psAll[:, 2048 + 128 * c:2048 + 128 * (c + 1)],
                                  x_land[:, c, 128:256], id_f[:])
            ins.then_inc(sm.h, 1)
        P.emit("tensor", f_tl)
        smtl1 = sm.inc(1)
        smtl2 = sm.inc(1)

        def f_tle(e, smtl1=smtl1, smtl2=smtl2):
            e.wait_ge(sm.h, smtl1)
            vstep(e, e.tensor_copy(xt1[:], psAll[:, 1024:2048]))
            e.wait_ge(sm.h, smtl2)
            e.tensor_copy(xt2[:], psAll[:, 2048:3072]).then_inc(sv.h, 1)
        svtle = sv.inc(1)
        P.emit("vector", f_tle)

        def f_sql(e, svtle=svtle):
            e.wait_ge(sv.h, svtle)
            e.activation(sq_l[:], xt1[:], AF.Square).then_inc(sa.h, 1)
        sa1 = sa.inc(1)
        P.emit("scalar", f_sql)

        def f_lnl(e, sa1=sa1, src=xt1, dst=xlr1, npart=128):
            e.wait_ge(sa.h, sa1)
            e.reduce_sum(stt[0:npart, 0:1], src[0:npart, :], axis=AX.X)
            vstep(e, e.reduce_sum(stt[0:npart, 1:2], sq_l[0:npart, :], axis=AX.X))
            vstep(e, e.tensor_scalar(stt[0:npart, 0:2], stt[0:npart, 0:2],
                                     1.0 / D, None, ALU.mult))
            vstep(e, e.tensor_tensor(stt[0:npart, 2:3], stt[0:npart, 0:1],
                                     stt[0:npart, 0:1], ALU.mult))
            vstep(e, e.tensor_tensor(stt[0:npart, 2:3], stt[0:npart, 1:2],
                                     stt[0:npart, 2:3], ALU.subtract))
            e.nop().then_inc(sv.h, 1)
        sv.inc(1)
        P.emit("vector", f_lnl)

        def f_sdl(e, v=sv.n):
            e.wait_ge(sv.h, v)
            e.activation(stt[:, 3:4], stt[:, 2:3], AF.Sqrt,
                         bias=eps_col[:]).then_inc(sa.h, 1)
        sa2 = sa.inc(1)
        P.emit("scalar", f_sdl)

        def f_apl(e, sa2=sa2, src=xt1, dst=xlr1, npart=128):
            e.wait_ge(sa.h, sa2)
            vstep(e, e.reciprocal(stt[0:npart, 4:5], stt[0:npart, 3:4]))
            vstep(e, e.tensor_tensor(stt[0:npart, 5:6], stt[0:npart, 0:1],
                                     stt[0:npart, 4:5], ALU.mult))
            e.tensor_scalar(dst[0:npart, :], src[0:npart, :], stt[0:npart, 4:5],
                            stt[0:npart, 5:6], ALU.mult,
                            ALU.subtract).then_inc(sv.h, 1)
        sv.inc(1)
        P.emit("vector", f_apl)

        def f_sql2(e, v=sv.n):
            e.wait_ge(sv.h, v)
            e.activation(sq_l[0:72, :], xt2[0:72, :], AF.Square).then_inc(sa.h, 1)
        sa3 = sa.inc(1)
        P.emit("scalar", f_sql2)

        def f_lnl2(e, sa3=sa3):
            e.wait_ge(sa.h, sa3)
            e.reduce_sum(stt[0:72, 0:1], xt2[0:72, :], axis=AX.X)
            vstep(e, e.reduce_sum(stt[0:72, 1:2], sq_l[0:72, :], axis=AX.X))
            vstep(e, e.tensor_scalar(stt[0:72, 0:2], stt[0:72, 0:2],
                                     1.0 / D, None, ALU.mult))
            vstep(e, e.tensor_tensor(stt[0:72, 2:3], stt[0:72, 0:1],
                                     stt[0:72, 0:1], ALU.mult))
            vstep(e, e.tensor_tensor(stt[0:72, 2:3], stt[0:72, 1:2],
                                     stt[0:72, 2:3], ALU.subtract))
            e.nop().then_inc(sv.h, 1)
        sv.inc(1)
        P.emit("vector", f_lnl2)

        def f_sdl2(e, v=sv.n):
            e.wait_ge(sv.h, v)
            e.activation(stt[0:72, 3:4], stt[0:72, 2:3], AF.Sqrt,
                         bias=eps_col[0:72, :]).then_inc(sa.h, 1)
        sa4 = sa.inc(1)
        P.emit("scalar", f_sdl2)

        def f_apl2(e, sa4=sa4):
            e.wait_ge(sa.h, sa4)
            vstep(e, e.reciprocal(stt[0:72, 4:5], stt[0:72, 3:4]))
            vstep(e, e.tensor_tensor(stt[0:72, 5:6], stt[0:72, 0:1],
                                     stt[0:72, 4:5], ALU.mult))
            e.tensor_scalar(xlr2[0:72, :], xt2[0:72, :], stt[0:72, 4:5],
                            stt[0:72, 5:6], ALU.mult,
                            ALU.subtract).then_inc(sv.h, 1)
        svland = sv.inc(1)
        P.emit("vector", f_apl2)

        for h in range(NH):
            c, b64 = h // 2, (h % 2) * 64
            for qi, (W1, W2) in enumerate([(WeT1, WeT2), (WrT1, WrT2)]):
                def f_we(e, h=h, qi=qi, W1=W1, W2=W2, svland=svland, gld=ld_g2.n,
                         gsv=sv.n, gc=state["consts_l1"]):
                    e.wait_ge(sv.h, svland)
                    e.wait_ge(ld_g2.h, gld)
                    if h == 0 and qi == 0:
                        for (s_, v_) in gc:
                            e.wait_ge(s_.h, v_)
                    else:
                        e.wait_ge(sv.h, gsv)
                    e.matmul(psA[0:20, 0:64], W1[:, h, :], xlr1[:, 64 * h:64 * h + 64]
                             if False else xlr1[:].rearrange("p (c q) -> p c q", q=64)[:, h, :],
                             start=True, stop=False, skip_group_check=True)
                    e.matmul(psA[0:20, 0:64], W2[:, h, :],
                             xlr2[0:72, :].rearrange("p (c q) -> p c q", q=64)[:, h, :],
                             start=False, stop=True,
                             skip_group_check=True).then_inc(sm.h, 1)
                smw = sm.inc(1)
                P.emit("tensor", f_we)

                def f_wee(e, qi=qi, smw=smw):
                    e.wait_ge(sm.h, smw)
                    e.tensor_copy(wesb[:, qi, :], psA[0:20, 0:64]).then_inc(sv.h, 1)
                svw = sv.inc(1)
                P.emit("vector", f_wee)

                def f_wf(e, qi=qi, svw=svw, wc=(wct if qi == 0 else wcb)):
                    e.wait_ge(sv.h, svw)
                    e.matmul(psB[0:64, 0:64], wc[:], wesb[:, qi, :],
                             start=True, stop=True,
                             skip_group_check=True).then_inc(sm.h, 1)
                smw2 = sm.inc(1)
                P.emit("tensor", f_wf)

                def f_wfe(e, qi=qi, smw2=smw2):
                    e.wait_ge(sm.h, smw2)
                    e.tensor_copy(wfsb[0:64, qi, :], psB[0:64, 0:64]).then_inc(sv.h, 1)
                svw2 = sv.inc(1)
                P.emit("vector", f_wfe)

                def f_wt(e, h=h, qi=qi, c=c, b64=b64, svw2=svw2):
                    e.wait_ge(sv.h, svw2)
                    e.transpose(psb16[b64:b64 + 64, 1024:1088], wfsb[0:64, qi, :],
                                id_b[0:64, 0:64],
                                tile_position=(0, b64)).then_inc(sm.h, 1)
                smw3 = sm.inc(1)
                P.emit("tensor", f_wt)

                def f_wte(e, h=h, qi=qi, c=c, b64=b64, smw3=smw3):
                    e.wait_ge(sm.h, smw3)
                    e.tensor_copy(weff[b64:b64 + 64, c, qi, :],
                                  psb16[b64:b64 + 64, 1024:1088]).then_inc(sv.h, 1)
                sv.inc(1)
                P.emit("vector", f_wte)

        state["weff_done"] = sv.n
        rows3 = AR.carve("rows3", [1, 4, 512], F32)
        pr_tm = AR.carve("pr_tm", [128, 2, 4, NE], F32)
        cwq_tm = AR.carve("cwq_tm", [128, 4, NE], F32)
        rg_sb = AR.carve("rg_sb", [8, 2, 512], F32)
        sq16 = AR.carve("sq16", [16, 2, 512], F32)
        rnb = AR.carve("rnb", [128, 512], F32)
        cur = AR.off
        AR.reset(ovq)
        q_t = AR.carve("q_t", [128, NCH, 512], BF16)
        k_t = AR.carve("k_t", [128, NCH, 512], BF16)
        assert AR.off <= ovx_end, (AR.off, ovx_end)
        AR.reset(cur)
        wring = AR.carve("wring", [128, 3, NCH, 128], BF16)
        ao_t = x1bf
        state["moe_base"] = ovq

        cw_tot = cast_w.n

        for t in range(4):
            ts_ = slice(512 * t, 512 * (t + 1))

            svg = sv.n

            def f_c1(e, ts_=ts_, svg=svg):
                e.wait_ge(sv.h, svg)
                e.tensor_copy(x1bf[:], x_res[:, :, ts_]).then_inc(sv.h, 1)
            svc = sv.inc(1)
            P.emit("vector", f_c1)

            def f_c2(e, svc=svc):
                e.wait_ge(sv.h, svc)
                e.activation(sqb[:], x1bf[:], AF.Square).then_inc(sa.h, 1)
            sasq = sa.inc(1)
            P.emit("scalar", f_c2)

            def f_mm(e, sasq=sasq, gc=consts):
                e.wait_ge(sa.h, sasq)
                if gc:
                    for (s_, v_) in gc:
                        e.wait_ge(s_.h, v_)
                for c in range(NCH):
                    i1 = e.matmul(psA[0:1, :], ones_bfc[:], x1bf[:, c, :],
                                  start=(c == 0), stop=(c == NCH - 1),
                                  skip_group_check=True)
                for c in range(NCH):
                    i1 = e.matmul(psB[0:1, :], ones_bfc[:], sqb[:, c, :],
                                  start=(c == 0), stop=(c == NCH - 1),
                                  skip_group_check=True)
                i1.then_inc(sm.h, 1)
            smn = sm.inc(1)
            consts = None
            P.emit("tensor", f_mm)

            def f_rw(e, smn=smn):
                e.wait_ge(sm.h, smn)
                vstep(e, e.tensor_scalar(rows3[0:1, 0, :], psA[0:1, :],
                                         1.0 / D, None, ALU.mult))
                vstep(e, e.tensor_scalar(rows3[0:1, 1, :], psB[0:1, :],
                                         1.0 / D, None, ALU.mult))
                vstep(e, e.tensor_tensor(rows3[0:1, 2, :], rows3[0:1, 0, :],
                                         rows3[0:1, 0, :], ALU.mult))
                e.tensor_tensor(rows3[0:1, 1, :], rows3[0:1, 1, :],
                                rows3[0:1, 2, :], ALU.subtract).then_inc(sv.h, 1)
            svv = sv.inc(1)
            P.emit("vector", f_rw)

            def f_sd(e, svv=svv):
                e.wait_ge(sv.h, svv)
                e.activation(rows3[0:1, 3, :], rows3[0:1, 1, :], AF.Sqrt,
                             bias=eps_col[0:1, :]).then_inc(sa.h, 1)
            sasd = sa.inc(1)
            P.emit("scalar", f_sd)

            def f_ab(e, sasd=sasd):
                e.wait_ge(sa.h, sasd)
                vstep(e, e.reciprocal(rows3[0:1, 1, :], rows3[0:1, 3, :]))
                e.tensor_tensor(rows3[0:1, 2, :], rows3[0:1, 0, :],
                                rows3[0:1, 1, :], ALU.mult).then_inc(sv.h, 1)
            svab = sv.inc(1)
            P.emit("vector", f_ab)

            def f_bc(e, svab=svab):
                e.wait_ge(sv.h, svab)
                e.matmul(psC, ones_f[:], rows3[0:1, 1, :], start=True, stop=True,
                         skip_group_check=True)
                e.matmul(psD, ones_f[:], rows3[0:1, 2, :], start=True, stop=True,
                         skip_group_check=True).then_inc(sm.h, 1)
            smbc = sm.inc(1)
            P.emit("tensor", f_bc)

            def f_ap(e, ts_=ts_, smbc=smbc):
                e.wait_ge(sm.h, smbc)
                vstep(e, e.tensor_copy(ab_bc[:], psC))
                vstep(e, e.tensor_copy(bb_bc[:], psD))
                vstep(e, e.tensor_tensor(
                    x1n[:, :, ts_], x_res[:, :, ts_],
                    ab_bc[:].unsqueeze(1).broadcast_to([128, NCH, 512]), ALU.mult))
                e.tensor_tensor(x1n[:, :, ts_], x1n[:, :, ts_],
                                bb_bc[:].unsqueeze(1).broadcast_to([128, NCH, 512]),
                                ALU.subtract).then_inc(sv.h, 1)
            svln = sv.inc(1)
            P.emit("vector", f_ap)

            def f_rt(e, ts_=ts_, svln=svln):
                e.wait_ge(sv.h, svln)
                for c in range(NCH):
                    e.matmul(psA[0:8, :], wr8[:, c, :], x1n[:, c, ts_],
                             start=(c == 0), stop=(c == NCH - 1),
                             skip_group_check=True)
                for c in range(NCH):
                    i1 = e.matmul(psB[0:8, :], wg8[:, c, :], x1n[:, c, ts_],
                                  start=(c == 0), stop=(c == NCH - 1),
                                  skip_group_check=True)
                i1.then_inc(sm.h, 1)
            smrt = sm.inc(1)
            P.emit("tensor", f_rt)

            def f_rte(e, smrt=smrt):
                e.wait_ge(sm.h, smrt)
                vstep(e, e.tensor_copy(rg_sb[:, 0, :], psA[0:8, :]))
                e.tensor_scalar(rg_sb[:, 1, :], psB[0:8, :], 10.0, None,
                                ALU.mult).then_inc(sv.h, 1)
            svrt = sv.inc(1)
            P.emit("vector", f_rte)

            def f_rtt(e, svrt=svrt):
                e.wait_ge(sv.h, svrt)
                for g in range(2):
                    for qd in range(4):
                        ins = e.transpose(psA.bitcast(F32)[:, 0:8] if False else
                                          psAll[:, 2048 + 8 * (4 * g + qd):
                                                2056 + 8 * (4 * g + qd)],
                                          rg_sb[0:8, g, 128 * qd:128 * (qd + 1)],
                                          id_f[0:8, 0:8])
                ins.then_inc(sm.h, 1)
            smtt = sm.inc(1)
            P.emit("tensor", f_rtt)

            def f_sf(e, t=t, smtt=smtt):
                e.wait_ge(sm.h, smtt)
                for g in range(2):
                    for qd in range(4):
                        reg = psAll[:, 2048 + 8 * (4 * g + qd):2056 + 8 * (4 * g + qd)]
                        vstep(e, e.tensor_copy(rg_tm[:, g, qd, :], reg))
                e.nop().then_inc(sv.h, 1)
            svtm = sv.inc(1)
            P.emit("vector", f_sf)

            for g in range(2):
                for qd in range(4):
                    def f_s1(e, g=g, qd=qd, gv=sv.n):
                        e.wait_ge(sv.h, gv)
                        e.reduce_max(stt[:, 0:1], rg_tm[:, g, qd, :], axis=AX.X,
                                     negate=True).then_inc(sv.h, 1)
                    svx = sv.inc(1)
                    P.emit("vector", f_s1)

                    def f_s2(e, g=g, qd=qd, svx=svx):
                        e.wait_ge(sv.h, svx)
                        e.activation(pr_tm[:, g, qd, :], rg_tm[:, g, qd, :], AF.Exp,
                                     bias=stt[:, 0:1],
                                     accum_out=stt[:, 1:2]).then_inc(sa.h, 1)
                    sax = sa.inc(1)
                    P.emit("scalar", f_s2)

                    def f_s3(e, g=g, qd=qd, sax=sax):
                        e.wait_ge(sa.h, sax)
                        vstep(e, e.reciprocal(stt[:, 2:3], stt[:, 1:2]))
                        e.tensor_scalar(pr_tm[:, g, qd, :], pr_tm[:, g, qd, :],
                                        stt[:, 2:3], None, ALU.mult).then_inc(sv.h, 1)
                    sv.inc(1)
                    P.emit("vector", f_s3)

            for qd in range(4):
                def f_cw(e, qd=qd, gv=sv.n):
                    e.wait_ge(sv.h, gv)
                    vstep(e, e.tensor_tensor(cwq_tm[:, qd, :], pr_tm[:, 0, qd, :],
                                             pr_tm[:, 1, qd, :], ALU.mult))
                    vstep(e, e.reduce_sum(stt[:, 3:4], cwq_tm[:, qd, :], axis=AX.X))
                    vstep(e, e.tensor_scalar(stt[:, 3:4], stt[:, 3:4], 1e-9,
                                             None, ALU.add))
                    vstep(e, e.reciprocal(stt[:, 4:5], stt[:, 3:4]))
                    e.tensor_scalar(cwq_tm[:, qd, :], cwq_tm[:, qd, :],
                                    stt[:, 4:5], None, ALU.mult).then_inc(sv.h, 1)
                sv.inc(1)
                P.emit("vector", f_cw)

            def f_ct(e, gv=sv.n):
                e.wait_ge(sv.h, gv)
                for qd in range(4):
                    ins = e.transpose(psAll[0:8, 2048 + 128 * qd:2048 + 128 * (qd + 1)],
                                      cwq_tm[:, qd, :], id_f[:])
                ins.then_inc(sm.h, 1)
            smct = sm.inc(1)
            P.emit("tensor", f_ct)

            def f_ce(e, t=t, smct=smct):
                e.wait_ge(sm.h, smct)
                for qd in range(4):
                    vstep(e, e.tensor_copy(
                        cw_fm[:, 512 * t + 128 * qd:512 * t + 128 * (qd + 1)],
                        psAll[0:8, 2048 + 128 * qd:2048 + 128 * (qd + 1)]))
                e.nop().then_inc(sv.h, 1)
            sv.inc(1)
            P.emit("vector", f_ce)

            for wi, (dst, bias) in enumerate([(q_t, bq_sb), (k_t, bk_sb)]):
                for oc in range(NCH):
                    lw = ld_w2.nxt()
                    buf = (wi * NCH + oc) % 3

                    def f_lq(e, wi=wi, oc=oc, buf=buf, lw=lw, gm=sm.n, cwt=cw_tot):
                        if gm:
                            e.wait_ge(sm.h, gm)
                        e.wait_ge(cast_w.h, cwt)
                        e.dma_start(out=wring[:, buf],
                                    in_=wqkv_bf[wi][:, 128 * oc:128 * (oc + 1)]
                                    .rearrange("(ic p) q -> p ic q", p=128)
                                    ).then_inc(lw[0].h, 16)
                    P.emit("sync", f_lq)

                    ps_q = psA if oc % 2 == 0 else psB

                    def f_mq(e, ts_=ts_, oc=oc, buf=buf, lw=lw, ps_q=ps_q,
                             svln=svln, gsa=sa.n):
                        e.wait_ge(lw[0].h, lw[1])
                        e.wait_ge(sv.h, svln)
                        if gsa:
                            e.wait_ge(sa.h, gsa)
                        for ic in range(NCH):
                            ins = e.matmul(ps_q, wring[:, buf, ic, :],
                                           x1n[:, ic, ts_], start=(ic == 0),
                                           stop=(ic == NCH - 1),
                                           skip_group_check=True)
                        ins.then_inc(sm.h, 1)
                    smq = sm.inc(1)
                    P.emit("tensor", f_mq)

                    def f_eq(e, dst=dst, bias=bias, oc=oc, ps_q=ps_q, smq=smq):
                        e.wait_ge(sm.h, smq)
                        e.activation(dst[:, oc, :], ps_q, AF.Identity,
                                     bias=bias[:, oc:oc + 1]).then_inc(sa.h, 1)
                    sa.inc(1)
                    P.emit("scalar", f_eq)

            for wi, dst in enumerate([q_t, k_t]):
                def f_nsq(e, dst=dst, gsa=sa.n, gsv=sv.n):
                    e.wait_ge(sa.h, gsa)
                    e.wait_ge(sv.h, gsv)
                    e.activation(sqb[:], dst[:], AF.Square).then_inc(sa.h, 1)
                sasq2 = sa.inc(1)
                P.emit("scalar", f_nsq)

                def f_nss(e, sasq2=sasq2):
                    e.wait_ge(sa.h, sasq2)
                    for c in range(NCH):
                        ins = e.matmul(psC[0:16, :], selq[:, c, :], sqb[:, c, :],
                                       start=(c == 0), stop=(c == NCH - 1),
                                       skip_group_check=True)
                    ins.then_inc(sm.h, 1)
                smss = sm.inc(1)
                P.emit("tensor", f_nss)

                def f_nrw(e, wi=wi, smss=smss):
                    e.wait_ge(sm.h, smss)
                    vstep(e, e.tensor_copy(sq16[:, 0, :], psC[0:16, :]))
                    e.nop().then_inc(sv.h, 1)
                svnr = sv.inc(1)
                P.emit("vector", f_nrw)

                def f_nsd(e, svnr=svnr):
                    e.wait_ge(sv.h, svnr)
                    e.activation(sq16[:, 1, :], sq16[:, 0, :],
                                 AF.Sqrt).then_inc(sa.h, 1)
                sansd = sa.inc(1)
                P.emit("scalar", f_nsd)

                def f_nrc(e, sansd=sansd):
                    e.wait_ge(sa.h, sansd)
                    vstep(e, e.tensor_scalar(sq16[:, 1, :], sq16[:, 1, :], 1e-12,
                                             None, ALU.max))
                    e.reciprocal(sq16[:, 0, :], sq16[:, 1, :]).then_inc(sv.h, 1)
                svrc = sv.inc(1)
                P.emit("vector", f_nrc)

                for c in range(NCH):
                    def f_nbc(e, c=c, svrc=svrc, gsv=sv.n):
                        e.wait_ge(sv.h, svrc)
                        if c > 0:
                            e.wait_ge(sv.h, gsv)
                        e.matmul(psD, selb[:, c, :], sq16[:, 0, :],
                                 start=True, stop=True,
                                 skip_group_check=True).then_inc(sm.h, 1)
                    smbc2 = sm.inc(1)
                    P.emit("tensor", f_nbc)

                    def f_nap(e, dst=dst, c=c, smbc2=smbc2):
                        e.wait_ge(sm.h, smbc2)
                        vstep(e, e.tensor_copy(rnb[:], psD))
                        e.tensor_tensor(dst[:, c, :], dst[:, c, :], rnb[:],
                                        ALU.mult).then_inc(sv.h, 1)
                    sv.inc(1)
                    P.emit("vector", f_nap)

            svqk = sv.n
            for c in range(NCH):
                def f_ho(e, c=c, svqk=svqk, gsa=sa.n, wd=state["weff_done"]):
                    e.wait_ge(sv.h, max(svqk, wd))
                    if gsa:
                        e.wait_ge(sa.h, gsa)
                    e.matmul(psD[0:64, :], weff[0:64, c, 0, :], q_t[0:64, c, :],
                             start=True, stop=False, skip_group_check=True)
                    e.matmul(psD[0:64, :], weff[0:64, c, 1, :], k_t[0:64, c, :],
                             start=False, stop=True, skip_group_check=True)
                    e.matmul(psD[64:128, :], weff[64:128, c, 0, :],
                             q_t[64:128, c, :], start=True, stop=False,
                             tile_position=(64, 64), skip_group_check=True)
                    e.matmul(psD[64:128, :], weff[64:128, c, 1, :],
                             k_t[64:128, c, :], start=False, stop=True,
                             tile_position=(64, 64),
                             skip_group_check=True).then_inc(sm.h, 1)
                smho = sm.inc(1)
                P.emit("tensor", f_ho)

                def f_hoe(e, c=c, smho=smho):
                    e.wait_ge(sm.h, smho)
                    e.activation(ao_t[:, c, :], psD, AF.Identity,
                                 bias=bc2[:]).then_inc(sa.h, 1)
                sa.inc(1)
                P.emit("scalar", f_hoe)

            for oc in range(NCH):
                lw = ld_w2.nxt()
                buf = (2 * NCH + oc) % 3

                def f_lo(e, oc=oc, buf=buf, lw=lw, gm=sm.n):
                    e.wait_ge(sm.h, gm)
                    e.dma_start(out=wring[:, buf],
                                in_=wqkv_bf[2][:, 128 * oc:128 * (oc + 1)]
                                .rearrange("(ic p) q -> p ic q", p=128)
                                ).then_inc(lw[0].h, 16)
                P.emit("sync", f_lo)

                ps_q = psA if oc % 2 == 0 else psB

                def f_mo(e, oc=oc, buf=buf, lw=lw, ps_q=ps_q, gsa=sa.n, gsv=sv.n):
                    e.wait_ge(lw[0].h, lw[1])
                    e.wait_ge(sa.h, gsa)
                    if gsv:
                        e.wait_ge(sv.h, gsv)
                    for ic in range(NCH):
                        ins = e.matmul(ps_q, wring[:, buf, ic, :], ao_t[:, ic, :],
                                       start=(ic == 0), stop=(ic == NCH - 1),
                                       skip_group_check=True)
                    ins.then_inc(sm.h, 1)
                smo = sm.inc(1)
                P.emit("tensor", f_mo)

                def f_ao(e, ts_=ts_, oc=oc, ps_q=ps_q, smo=smo):
                    e.wait_ge(sm.h, smo)
                    e.scalar_tensor_tensor(x_res[:, oc, ts_], ps_q,
                                           bo_sb[:, oc:oc + 1], x_res[:, oc, ts_],
                                           ALU.add, ALU.add).then_inc(sv.h, 1)
                sv.inc(1)
                P.emit("vector", f_ao)

        dma_sh.pend(P, "gpsimd")
        ld_w2.pend(P, "sync")
        ld_c.pend(P, "sync")
        P.pend("vector", sv)
        P.pend("tensor", sm)
        P.pend("scalar", sa)
        P.pend("gpsimd", ld_g)

    def q2():
        psC = psAll[:, 1024:1536]
        psD = psAll[:, 1536:2048]
        x1n = state["x1n"]
        AR.reset(state["moe_base"])
        h_moe = AR.carve("h_moe", [128, HCH, 512], BF16)
        gsb = AR.carve("gsb", [128, 3, NCH, 128], BF16)
        usb = AR.carve("usb", [128, 3, NCH, 128], BF16)
        dsb = AR.carve("dsb", [128, 3, D], BF16)
        silu_sb = AR.carve("silu_sb", [128, 2, 512], F32)
        cwb = AR.carve("cwb", [128, 512], F32)
        cw1 = AR.carve("cw1", [1, 512], F32)
        tmp8 = AR.carve("tmp8", [128, 2, 512], F32)

        g_ms = {}
        d_ms = {}

        for ei in range(8):
            for tt in range(4):
                tt_ = slice(512 * tt, 512 * (tt + 1))
                shv = dma_sh.nxt()

                def f_sh(e, ei=ei, tt=tt, shv=shv, gsv=sv.n, gsm=sm.n):
                    e.wait_ge(sv.h, gsv)
                    if gsm:
                        e.wait_ge(sm.h, gsm)
                    e.dma_start(out=cw1[:],
                                in_=cw_fm[ei:ei + 1, 512 * tt:512 * (tt + 1)]
                                ).then_inc(shv[0].h, 16)
                P.emit("gpsimd", f_sh)

                def f_cb(e, ei=ei, tt=tt, shv=shv, gd=sm.n, gv=sv.n):
                    e.wait_ge(shv[0].h, shv[1])
                    if gd:
                        e.wait_ge(sm.h, gd)
                    if gv:
                        e.wait_ge(sv.h, gv)
                    e.matmul(psAll[:, 3584:4096], ones_f[:],
                             cw1[0:1, 0:512], start=True, stop=True,
                             skip_group_check=True).then_inc(sm.h, 1)
                smcb = sm.inc(1)
                P.emit("tensor", f_cb)

                def f_cbe(e, smcb=smcb):
                    e.wait_ge(sm.h, smcb)
                    e.tensor_copy(cwb[:], psAll[:, 3584:4096]).then_inc(sv.h, 1)
                svcb = sv.inc(1)
                P.emit("vector", f_cbe)

                g0sa, g0sv = sa.n, sv.n
                slh_ms, hh_ms = {}, {}
                for hc in range(HCH):
                    buf = hc % 3
                    prev3 = max(x for x in range(HCH) if x % 3 == hc % 3)
                    lg = ld_m.nxt()
                    lu = ld_d.nxt()
                    if hc >= 3:
                        guard = g_ms.get((ei, tt, hc - 3), 0)
                    else:
                        guard = g_ms.get((ei, tt - 1, prev3), 0) \
                            or g_ms.get((ei - 1, 3, prev3), 0)

                    def f_lg(e, ei=ei, hc=hc, buf=buf, lg=lg, lu=lu, guard=guard,
                             cst=cast_s[ei]):
                        e.wait_ge(cst.h, 48)
                        if guard:
                            e.wait_ge(sm.h, guard)
                        e.dma_start(out=gsb[:, buf],
                                    in_=gate_bf[ei][:, 128 * hc:128 * (hc + 1)]
                                    .rearrange("(ic p) q -> p ic q", p=128)
                                    ).then_inc(lg[0].h, 16)
                        e.dma_start(out=usb[:, buf],
                                    in_=up_bf[ei][:, 128 * hc:128 * (hc + 1)]
                                    .rearrange("(ic p) q -> p ic q", p=128)
                                    ).then_inc(lu[0].h, 16)
                    P.emit("sync", f_lg)

                    pg = psA if hc % 2 == 0 else psC
                    pu = psB if hc % 2 == 0 else psD
                    gsa2 = slh_ms.get(hc - 2, g0sa)
                    gsv2_ = hh_ms.get(hc - 2, g0sv)

                    def f_mg(e, tt_=tt_, buf=buf, lg=lg, lu=lu,
                             pg=pg, pu=pu, gsa=gsa2, gsv=gsv2_):
                        e.wait_ge(lg[0].h, lg[1])
                        e.wait_ge(lu[0].h, lu[1])
                        e.wait_ge(sv.h, gsv)
                        e.wait_ge(sa.h, gsa)
                        for ic in range(NCH):
                            e.matmul(pg, gsb[:, buf, ic, :], x1n[:, ic, tt_],
                                     start=(ic == 0), stop=(ic == NCH - 1),
                                     skip_group_check=True)
                        for ic in range(NCH):
                            ins = e.matmul(pu, usb[:, buf, ic, :], x1n[:, ic, tt_],
                                           start=(ic == 0), stop=(ic == NCH - 1),
                                           skip_group_check=True)
                        ins.then_inc(sm.h, 1)
                    g_ms[(ei, tt, hc)] = sm.inc(1)
                    P.emit("tensor", f_mg)

                    def f_sl(e, hc=hc, smv=g_ms[(ei, tt, hc)], pg=pg):
                        e.wait_ge(sm.h, smv)
                        e.activation(silu_sb[:, hc % 2], pg,
                                     AF.Silu).then_inc(sa.h, 1)
                    sasl = sa.inc(1)
                    slh_ms[hc] = sasl
                    P.emit("scalar", f_sl)

                    def f_h(e, hc=hc, sasl=sasl, svcb=svcb, pu=pu):
                        e.wait_ge(sa.h, sasl)
                        e.wait_ge(sv.h, svcb)
                        e.tensor_tensor(h_moe[:, hc, :], pu,
                                        silu_sb[:, hc % 2],
                                        ALU.mult).then_inc(sv.h, 1)
                    hh_ms[hc] = sv.inc(1)
                    P.emit("vector", f_h)

                h_done = sv.n
                for kc in range(HCH):
                    buf = kc % 3
                    prev3 = max(x for x in range(HCH) if x % 3 == kc % 3)
                    ld_ = ld_dn.nxt()
                    if kc >= 3:
                        guard = d_ms.get((ei, tt, kc - 3), 0)
                    else:
                        guard = d_ms.get((ei, tt - 1, prev3), 0) \
                            or d_ms.get((ei - 1, 3, prev3), 0)

                    def f_ldn(e, ei=ei, kc=kc, buf=buf, ld_=ld_, guard=guard,
                              cst=cast_s[ei]):
                        e.wait_ge(cst.h, 48)
                        if guard:
                            e.wait_ge(sm.h, guard)
                        e.dma_start(out=dsb[:, buf],
                                    in_=down_bf[ei][128 * kc:128 * (kc + 1), :]
                                    ).then_inc(ld_[0].h, 16)
                    P.emit("sync", f_ldn)

                    def f_md(e, ei=ei, tt=tt, kc=kc, buf=buf, ld_=ld_,
                             h_done=h_done, gsv3=sv.n):
                        e.wait_ge(ld_[0].h, ld_[1])
                        if kc == 0:
                            e.wait_ge(sv.h, h_done)
                        for oc in range(NCH):
                            ins = e.matmul(psD8[:, oc, :],
                                           dsb[:, buf, 128 * oc:128 * (oc + 1)],
                                           h_moe[:, kc, :],
                                           start=(kc == 0), stop=(kc == HCH - 1),
                                           skip_group_check=True)
                        ins.then_inc(sm.h, 1)
                    d_ms[(ei, tt, kc)] = sm.inc(1)
                    P.emit("tensor", f_md)

                def f_de(e, ei=ei, tt_=tt_, smv=d_ms[(ei, tt, HCH - 1)], tt=tt):
                    e.wait_ge(sm.h, smv)
                    for gg in range(4):
                        vstep(e, e.tensor_tensor(
                            tmp8[:],
                            psAll[:, 1024 * gg:1024 * (gg + 1)]
                            .rearrange("p (a q) -> p a q", q=512),
                            cwb[:].unsqueeze(1).broadcast_to([128, 2, 512]),
                            ALU.mult))
                        ts2 = slice(512 * tt, 512 * (tt + 1))
                        vstep(e, e.tensor_tensor(
                            x_res[:, 2 * gg:2 * (gg + 1), ts2],
                            x_res[:, 2 * gg:2 * (gg + 1), ts2], tmp8[:], ALU.add))
                    e.nop().then_inc(sv.h, 1)
                sv.inc(1)
                P.emit("vector", f_de)

        ld_m.pend(P, "sync")
        ld_d.pend(P, "sync")
        ld_dn.pend(P, "sync")
        ld_w2.pend(P, "sync")
        dma_sh.pend(P, "gpsimd")
        for cs in cast_s:
            P.pend("gpsimd", cs)
        P.pend("vector", sv)
        P.pend("tensor", sm)
        P.pend("scalar", sa)


    q0()
    if "p0" in phases:
        p0()
    P.barrier()
    if "p1" in phases:
        p1()
    P.barrier()
    if "p2" in phases:
        p2()
    P.barrier()
    if "p3" in phases:
        p3()
    P.barrier()
    if "p4" in phases:
        p4()
    P.barrier()
    if "l1" in phases or "q1" in phases:
        q1()
        P.barrier()
        if "l1" in phases:
            q2()
            P.barrier()

        fs = sg.inc(16)

        def f_out(e, fs=fs):
            e.dma_start(out=x2_out, in_=x_res[:]).then_inc(sg.h, 16)
            e.wait_ge(sg.h, fs)
        P.emit("gpsimd", f_out)

    specs = {
        "x_res": ([128, NCH, T], F32, lambda: x_res[:]),
        "x_land": ([128, NCH, EXT], F32, lambda: x_land[:]),
        "x1n": ([128, NCH, T], BF16, lambda: state["x1n"][:]),
        "cw_fm": ([8, T], F32, lambda: cw_fm[:]),
        "attn_h": ([128, NCH, TQ], BF16, lambda: state["attn_h"][:]),
        "f_row": ([1, XQW], F32, lambda: f_row[:]),
    }
    outs = []
    for name in dbg:
        if name in specs:
            shape, dt, get = specs[name]
            dd = nc.dram_tensor("dbg_" + name, shape, dt, kind="ExternalOutput").ap()
            outs.append((dd, get))
    if outs:
        v0 = sg.n

        def f_gd(e, outs=outs, v0=v0):
            for (dram, get) in outs:
                e.dma_start(out=dram, in_=get()).then_inc(sg.h, 16)
            e.wait_ge(sg.h, v0 + 16 * len(outs))
        P.emit("gpsimd", f_gd)
        sg.inc(16 * len(outs))

    P.finalize()
    ex.close()
    return nc


LM_IDX = np.linspace(0.0, SEQ - 1, LEN_L).astype(np.int32)


def _mk_consts_l1():
    selq = np.zeros((128, NCH, 16), dtype=BFnp)
    selb = np.zeros((16, NCH, 128), dtype=np.float32)
    for c in range(NCH):
        selq[0:64, c, 2 * c] = 1
        selq[64:128, c, 2 * c + 1] = 1
        selb[2 * c, c, 0:64] = 1
        selb[2 * c + 1, c, 64:128] = 1
    return selq, selb


def _fused_in_map(inputs, core):
    b, half = core // 2, core % 2
    c = _mk_consts()
    selq, selb = _mk_consts_l1()
    x_seq = np.asarray(inputs["L0_attn_x"][b])
    cw_taps = np.asarray(inputs["L0_ffn_conv_w"])[:, 0, :]
    if half == 1:
        x_seq = x_seq[::-1]
        cw_taps = cw_taps[:, ::-1]
    x_seq = np.ascontiguousarray(x_seq)

    if half == 0:
        vidx = LM_IDX.astype(np.int64)
    else:
        vidx = (SEQ - 1 - LM_IDX).astype(np.int64)
    xe = np.zeros((EXTW, D), dtype=np.float32)
    xe[0:LEN_L] = x_seq[vidx]
    lm = vidx - 1
    ok = lm >= 0
    xe[256:256 + LEN_L][ok] = x_seq[np.clip(lm, 0, SEQ - 1)][ok]
    rm = vidx + 1
    ok = rm < SEQ
    xe[512:512 + LEN_L][ok] = x_seq[np.clip(rm, 0, SEQ - 1)][ok]
    xe_fm = np.zeros((128, NCH, EXT), dtype=np.float32)
    xe_fm[:, :, 0:LEN_L] = (xe[0:LEN_L].T.reshape(NCH, 128, LEN_L)
                            .transpose(1, 0, 2))

    d = {
        "x_seq": x_seq,
        "x_extra": xe,
        "x_extra_fm": np.ascontiguousarray(xe_fm),
        "idb": c["idb"], "idf": c["idf"], "ones_f_row": c["ones_f_row"],
        "w1T": np.ascontiguousarray(np.asarray(inputs["L0_ffn_w1"]).T),
        "w2T": np.ascontiguousarray(np.asarray(inputs["L0_ffn_w2"]).T),
        "convw": np.ascontiguousarray(
            cw_taps.reshape(32, 128, 3).transpose(1, 0, 2).reshape(128, 96)),
        "convb": np.ascontiguousarray(
            np.asarray(inputs["L0_ffn_conv_b"]).reshape(32, 128).T),
        "selq": selq, "selb": selb,
        "Wq": np.asarray(inputs["L1_attn_Wq"]),
        "Wk": np.asarray(inputs["L1_attn_Wk"]),
        "Wo": np.asarray(inputs["L1_attn_Wo"]),
        "bq": np.ascontiguousarray(np.asarray(inputs["L1_attn_bq"]).reshape(NCH, 128).T),
        "bk": np.ascontiguousarray(np.asarray(inputs["L1_attn_bk"]).reshape(NCH, 128).T),
        "bo": np.ascontiguousarray(np.asarray(inputs["L1_attn_bo"]).reshape(NCH, 128).T),
        "bc2": np.ascontiguousarray(np.tile(np.asarray(inputs["L1_attn_bc"]), 2)[:, None]),
        "WeT": np.ascontiguousarray(np.asarray(inputs["L1_attn_We"]).transpose(1, 0, 2)),
        "WrT": np.ascontiguousarray(np.asarray(inputs["L1_attn_Wr"]).transpose(1, 0, 2)),
        "Wc_top": np.ascontiguousarray(np.asarray(inputs["L1_attn_Wc"])[0:20]),
        "Wc_bot": np.ascontiguousarray(np.asarray(inputs["L1_attn_Wc"])[20:40]),
        "w_router": np.asarray(inputs["L1_ffn_w_router"]),
        "w_gating": np.asarray(inputs["L1_ffn_w_gating"]),
        "w_gate": np.asarray(inputs["L1_ffn_w_gate"]),
        "w_up": np.asarray(inputs["L1_ffn_w_up"]),
        "w_down": np.asarray(inputs["L1_ffn_w_down"]),
    }
    for p in "qkvo":
        d[f"basT_{p}"] = np.ascontiguousarray(np.asarray(inputs[f"L0_attn_basis_{p}"]).T)
        d[f"phT_{p}"] = np.ascontiguousarray(np.asarray(inputs[f"L0_attn_phase_{p}"]).T)
        d[f"ampT_{p}"] = np.ascontiguousarray(np.asarray(inputs[f"L0_attn_amp_{p}"]).T)
    return d


def _x_tok(x_fm):
    return x_fm.transpose(1, 0, 2).reshape(D, T).T


def kernel(**inputs):
    nc = build_fused()
    maps = [_fused_in_map(inputs, core) for core in range(8)]
    res = run_bass_kernel_spmd(nc, maps, core_ids=list(range(8)))
    out = np.empty((4, SEQ, D), dtype=np.float32)
    for core in range(8):
        b, half = core // 2, core % 2
        v = res.results[core]["x2_fm"]
        if half == 1:
            v = v[:, :, ::-1]
        out[b, half * T:(half + 1) * T] = _x_tok(v)
    return out
